# version 17
# baseline (speedup 1.0000x reference)
import contextlib
import numpy as np
import ml_dtypes
import concourse.bass as bass
import concourse.mybir as mybir
from concourse.bass_utils import run_bass_kernel_spmd

F32 = mybir.dt.float32
BF16 = mybir.dt.bfloat16
AF = mybir.ActivationFunctionType
ALU = mybir.AluOpType
AXX = mybir.AxisListType.X

NEG = -30000.0
BIGL = 1.0e9


def make_cfg(T=4096, D=4096):
    H = 4
    cfg = dict(T=T, D=D, H=H, HD=D // H)
    cfg["HC"] = D // 128
    cfg["CPH"] = cfg["HD"] // 128
    cfg["NSEQ"] = T // 128
    cfg["TH"] = T // 2
    cfg["NJ"] = cfg["TH"] // 128
    cfg["GS"] = 4
    assert cfg["NJ"] % cfg["GS"] == 0
    cfg["NGRP"] = cfg["NJ"] // cfg["GS"]
    cfg["M"] = cfg["TH"] // 128
    cfg["NMASK"] = 2 * cfg["GS"]
    cfg["NE"] = 28
    cfg["RANK"] = 8
    cfg["NER"] = 224
    cfg["NARE"] = 252
    cfg["OGW"] = 256
    cfg["NOG"] = D // cfg["OGW"]
    cfg["NDH"] = max(1, cfg["HD"] // 512)
    cfg["DHW"] = min(512, cfg["HD"])
    return cfg



def local_to_global_rows(cfg, parity):
    TH, NJ = cfg["TH"], cfg["NJ"]
    gmap = np.zeros(TH, dtype=np.int64)
    for j in range(NJ):
        for p in range(64):
            gmap[j * 128 + p] = 128 * j + 64 * parity + p
            gmap[j * 128 + 64 + p] = TH + 128 * j + 64 * parity + p
    return gmap


def build_masks(cfg, parity):
    GS, M = cfg["GS"], cfg["M"]
    out = np.zeros((cfg["NMASK"], 128, GS * 128), dtype=np.float32)
    s = np.arange(128)[:, None]
    p = np.arange(128)[None, :]
    lower = p < 64
    upper = ~lower
    for idx in range(cfg["NMASK"]):
        midx = idx if idx < GS else M + (idx - GS)
        for jj in range(GS):
            d = midx - jj
            if d < 0:
                continue
            blk = out[idx][:, jj * 128:(jj + 1) * 128]
            if d == 0:
                qrel = 64 * parity + p
                blk += np.where(lower & (s > qrel), NEG, 0.0)
            elif d < M:
                blk += np.where(lower, NEG, 0.0)
            elif d == M:
                qrel = 64 * parity + (p - 64)
                blk += np.where(lower | (upper & (s > qrel)), NEG, 0.0)
            else:
                blk += NEG
    return np.clip(out, NEG, 0.0).astype(ml_dtypes.bfloat16)


def bv_rlayout(bv):
    flat = np.ascontiguousarray(bv, dtype=np.float32).reshape(-1)
    return np.ascontiguousarray(flat.reshape(-1, 128).T)


def stage_inputs(cfg, x, bvq, bvk, bvv, W, gateW, loraA, loraB):
    D = cfg["D"]
    wt = np.ascontiguousarray(np.asarray(W).T, dtype=np.float32)
    A2 = np.ascontiguousarray(np.asarray(loraA).reshape(cfg["NE"] * cfg["RANK"], D))
    ag = np.ascontiguousarray(
        np.concatenate([np.asarray(gateW).T, A2.T], axis=1), dtype=np.float32)
    B2 = np.ascontiguousarray(
        np.transpose(np.asarray(loraB), (0, 2, 1)).reshape(cfg["NER"], D))
    bq, bk, bv_ = bv_rlayout(bvq), bv_rlayout(bvk), bv_rlayout(bvv)
    in_maps, gmaps = [], []
    for c in range(8):
        b, parity = c // 2, c & 1
        bb = min(b, x.shape[0] - 1)
        gmap = local_to_global_rows(cfg, parity)
        gmaps.append((bb, gmap))
        in_maps.append({
            "xs": np.ascontiguousarray(x[bb], dtype=np.float32),
            "xq": np.ascontiguousarray(x[bb][gmap], dtype=np.float32),
            "gidx": np.ascontiguousarray(gmap.reshape(cfg["NJ"], 128).T.astype(np.uint32)),
            "masks": build_masks(cfg, parity),
            "bvq": bq, "bvk": bk, "bvv": bv_,
            "wt": wt, "ag": ag,
            "b2a": np.ascontiguousarray(B2[0:112], dtype=np.float32),
            "b2b": np.ascontiguousarray(B2[112:224], dtype=np.float32),
        })
    return in_maps, gmaps



class Sem:
    def __init__(self, handle):
        self.h = handle
        self.n = 0

    def inc(self, instr, k):
        instr.then_inc(self.h, k)
        self.n += k
        return instr

    def wait(self, eng, val=None):
        v = self.n if val is None else val
        if v > 0:
            eng.wait_ge(self.h, v)


def build_program(cfg):
    T, D, H = cfg["T"], cfg["D"], cfg["H"]
    HC, CPH, HD = cfg["HC"], cfg["CPH"], cfg["HD"]
    NSEQ, TH, NJ = cfg["NSEQ"], cfg["TH"], cfg["NJ"]
    GS, NGRP, M, NMASK = cfg["GS"], cfg["NGRP"], cfg["M"], cfg["NMASK"]
    NE, NER, NARE = cfg["NE"], cfg["NER"], cfg["NARE"]
    OGW, NOG = cfg["OGW"], cfg["NOG"]
    NDH, DHW = cfg["NDH"], cfg["DHW"]
    NTOT = NSEQ + NJ
    NTB = NSEQ // 4
    HCG = HC // 4
    CPB = max(1, HC // 8)
    PSA_W = GS * 128
    NKmax = M + GS * NGRP
    NU = H * NGRP

    def NK(g):
        return M + GS * (g + 1)

    nc = bass.Bass("TRN2", target_bir_lowering=False)

    xs = nc.dram_tensor("xs", [T, D], F32, kind="ExternalInput").ap()
    xq = nc.dram_tensor("xq", [TH, D], F32, kind="ExternalInput").ap()
    gidx_in = nc.dram_tensor("gidx", [128, NJ], mybir.dt.uint32, kind="ExternalInput").ap()
    masks_in = nc.dram_tensor("masks", [NMASK, 128, PSA_W], BF16, kind="ExternalInput").ap()
    bvq_in = nc.dram_tensor("bvq", [128, HC], F32, kind="ExternalInput").ap()
    bvk_in = nc.dram_tensor("bvk", [128, HC], F32, kind="ExternalInput").ap()
    bvv_in = nc.dram_tensor("bvv", [128, HC], F32, kind="ExternalInput").ap()
    wt_in = nc.dram_tensor("wt", [D, D], F32, kind="ExternalInput").ap()
    ag_in = nc.dram_tensor("ag", [D, NARE], F32, kind="ExternalInput").ap()
    b2a_in = nc.dram_tensor("b2a", [112, D], F32, kind="ExternalInput").ap()
    b2b_in = nc.dram_tensor("b2b", [112, D], F32, kind="ExternalInput").ap()
    out_hbm = nc.dram_tensor("out", [TH, D], F32, kind="ExternalOutput").ap()

    xh_sc = nc.dram_tensor("xh_sc", [T, D], BF16, kind="Internal").ap()
    xq_sc = nc.dram_tensor("xq_sc", [TH, D], BF16, kind="Internal").ap()
    a_sc = nc.dram_tensor("a_sc", [TH, D], BF16, kind="Internal").ap()
    vb_sc = nc.dram_tensor("vb_sc", [1, D], BF16, kind="Internal").ap()

    with contextlib.ExitStack() as OS:
        def sb(stack, name, shp, dt=F32):
            return stack.enter_context(nc.sbuf_tensor(name, shp, dt))

        def ps(stack, name, shp):
            return stack.enter_context(nc.psum_tensor(name, shp, F32))

        ident = sb(OS, "ident", [128, 128], BF16)
        g_all = sb(OS, "g_all", [128, HC], F32)
        epsv = sb(OS, "epsv", [128, 1], F32)
        onesrow = sb(OS, "onesrow", [1, 128], BF16)
        P1_IDXT = [sb(OS, "idxt", [128, NJ], mybir.dt.uint32)]

        sem_names = [
            "s_x0", "s_x1", "s_x2", "s_x3", "s_x4", "s_x5", "s_x6", "s_x7",
            "s_msk", "s_qg", "s_yp0", "s_yp1", "s_ab0", "s_ab1",
            "s_vbf", "s_xw", "s_xw1", "s_yb0", "s_yb1", "s_yb2",
            "s_aw0", "s_aw1", "s_aw2", "s_aw3", "s_yw0", "s_yw1",
            "s_os0", "s_os1", "s_gW", "s_wt0", "s_wt1",
            "s_pe", "s_dve", "s_act", "s_pool",
        ]
        SS = {nm: Sem(OS.enter_context(nc.semaphore(nm))) for nm in sem_names}
        s_vtM = Sem(OS.enter_context(nc.semaphore("s_vtM")))
        s_ac = [Sem(OS.enter_context(nc.semaphore(f"s_ac{i}"))) for i in range(3)]
        s_gt = [Sem(OS.enter_context(nc.semaphore(f"s_gt{i}"))) for i in range(2)]
        s_xqw = Sem(OS.enter_context(nc.semaphore("s_xqw")))
        s_kt0 = Sem(OS.enter_context(nc.semaphore("s_kt0")))
        s_ktR = Sem(OS.enter_context(nc.semaphore("s_ktR")))
        s_vt0 = Sem(OS.enter_context(nc.semaphore("s_vt0")))
        s_vtR = Sem(OS.enter_context(nc.semaphore("s_vtR")))
        s_x = [SS[f"s_x{i}"] for i in range(8)]
        s_yp = [SS["s_yp0"], SS["s_yp1"]]
        s_ab = [SS["s_ab0"], SS["s_ab1"]]
        s_aw = [SS["s_aw0"], SS["s_aw1"], SS["s_aw2"], SS["s_aw3"]]
        s_yw = [SS["s_yw0"], SS["s_yw1"]]
        s_os = [SS["s_os0"], SS["s_os1"]]
        s_wt = [SS["s_wt0"], SS["s_wt1"]]
        s_msk, s_qg, s_xw = SS["s_msk"], SS["s_qg"], SS["s_xw"]
        s_xwp = [SS["s_xw"], SS["s_xw1"]]
        s_yb = [SS["s_yb0"], SS["s_yb1"], SS["s_yb2"]]
        s_gW, s_vbf = SS["s_gW"], SS["s_vbf"]
        pe, dve, act, pool = SS["s_pe"], SS["s_dve"], SS["s_act"], SS["s_pool"]

        block = OS.enter_context(nc.Block())

        with contextlib.ExitStack() as P0:
            iota0 = sb(P0, "iota0", [128, 128], F32)
            ioh = sb(P0, "ioh", [4, HC], F32)
            oh_a = sb(P0, "oh_a", [4, HC], F32)
            oh_b = sb(P0, "oh_b", [4, HC], F32)
            onehot = sb(P0, "onehot", [4, HC], BF16)
            ones4 = sb(P0, "ones4", [4, 128], F32)
            onescol = sb(P0, "onescol", [128, 1], BF16)
            bvqf = sb(P0, "bvqf", [128, HC], F32)
            bvkf = sb(P0, "bvkf", [128, HC], F32)
            bvvf = sb(P0, "bvvf", [128, HC], F32)
            absb = sb(P0, "absb", [128, 12], BF16)
            absr = sb(P0, "absr", [128, 12], F32)
            alphas = sb(P0, "alphas", [4, 3], F32)
            alrep = sb(P0, "alrep", [4, 128 * 3], BF16)
            abq = sb(P0, "abq", [128, HC], F32)
            abk = sb(P0, "abk", [128, HC], F32)
            abv = sb(P0, "abv", [128, HC], F32)
            sgn = sb(P0, "sgn", [128, HC], F32)
            zro = sb(P0, "zro", [128, HC], F32)
            bndq = sb(P0, "bndq", [128, HC], F32)
            bndk = sb(P0, "bndk", [128, HC], F32)
            bndv = sb(P0, "bndv", [128, HC], BF16)
            vbtmp = sb(P0, "vbtmp", [HC, 128], BF16)
            p0a = ps(P0, "p0a", [128, 512])
            p0b = ps(P0, "p0b", [128, 512])

            def vstep(v, ins):
                dve.inc(ins, 1)
                v.wait_ge(dve.h, dve.n)

            @block.gpsimd
            def _(gp):
                gp.iota(iota0[:], [[1, 128]], base=0, channel_multiplier=-1,
                        allow_small_or_imprecise_dtypes=True)
                gp.iota(ioh[:], [[1, HC]], base=0, channel_multiplier=-CPH,
                        allow_small_or_imprecise_dtypes=True)
                gp.memset(epsv[:], 1e-5)
                gp.memset(ones4[:], 1.0)
                gp.memset(onescol[:], 1.0)
                pool.inc(gp.memset(onesrow[:], 1.0), 1)

            @block.sync
            def _(sy):
                s_msk.inc(sy.dma_start(out=bvqf[:], in_=bvq_in), 16)
                s_msk.inc(sy.dma_start(out=bvkf[:], in_=bvk_in), 16)
                s_msk.inc(sy.dma_start(out=bvvf[:], in_=bvv_in), 16)
                s_msk.inc(sy.dma_start(out=P1_IDXT[0][:], in_=gidx_in), 16)

            @block.vector
            def _(v):
                pool.wait(v)
                s_msk.wait(v)
                vstep(v, v.tensor_scalar(ident[:], iota0[:], 0.0, None, op0=ALU.is_equal))
                vstep(v, v.tensor_scalar(oh_a[:], ioh[:], -0.5, None, op0=ALU.is_ge))
                vstep(v, v.tensor_scalar(oh_b[:], ioh[:], CPH - 0.5, None, op0=ALU.is_le))
                vstep(v, v.tensor_tensor(onehot[:], oh_a[:], oh_b[:], op=ALU.mult))
                for t, bvf in enumerate([bvqf, bvkf, bvvf]):
                    for h in range(4):
                        vstep(v, v.tensor_reduce(
                            absr[:, t * 4 + h:t * 4 + h + 1],
                            bvf[:, h * CPH:(h + 1) * CPH], axis=AXX, op=ALU.add,
                            apply_absolute_value=True))
                vstep(v, v.tensor_copy(absb[:], absr[:]))

            @block.tensor
            def _(t):
                t.wait_ge(dve.h, dve.n)
                for tt in range(3):
                    ins = t.matmul(p0a[0:4, tt:tt + 1], absb[:, tt * 4:(tt + 1) * 4],
                                   onescol[:], start=True, stop=True)
                pe.inc(ins, 1)

            @block.vector
            def _(v):
                v.wait_ge(pe.h, pe.n)
                vstep(v, v.tensor_scalar(alphas[:], p0a[0:4, 0:3], 1.0 / HD, None,
                                         op0=ALU.mult))
                for t in range(3):
                    vstep(v, v.tensor_scalar(alrep[:, t * 128:(t + 1) * 128], ones4[:],
                                             alphas[:, t:t + 1], None, op0=ALU.mult))

            @block.tensor
            def _(t):
                t.wait_ge(dve.h, dve.n)
                pe.inc(t.matmul(p0a[:, 0:HC], alrep[:, 0:128], onehot[:],
                                start=True, stop=True), 1)
                pe.inc(t.matmul(p0b[:, 0:HC], alrep[:, 128:256], onehot[:],
                                start=True, stop=True), 1)

            @block.vector
            def _(v):
                v.wait_ge(pe.h, pe.n)
                vstep(v, v.tensor_copy(abq[:], p0a[:, 0:HC]))
                vstep(v, v.tensor_copy(abk[:], p0b[:, 0:HC]))

            @block.tensor
            def _(t):
                t.wait_ge(dve.h, dve.n)
                pe.inc(t.matmul(p0a[:, 0:HC], alrep[:, 256:384], onehot[:],
                                start=True, stop=True), 1)

            @block.vector
            def _(v):
                v.wait_ge(pe.h, pe.n)
                vstep(v, v.tensor_copy(abv[:], p0a[:, 0:HC]))

            for bvf, ab, out_bind, last in [(bvqf, abq, bndq, False),
                                            (bvkf, abk, bndk, True),
                                            (bvvf, abv, bndv, None)]:
                @block.scalar
                def _(s, bvf=bvf):
                    s.wait_ge(dve.h, dve.n)
                    act.inc(s.activation(sgn[:], bvf[:], AF.Sign), 1)

                @block.vector
                def _(v, bvf=bvf, ab=ab, out_bind=out_bind, last=last):
                    v.wait_ge(act.h, act.n)
                    vstep(v, v.tensor_scalar(zro[:], bvf[:], 0.0, None, op0=ALU.is_equal))
                    vstep(v, v.tensor_tensor(sgn[:], sgn[:], zro[:], op=ALU.add))
                    vstep(v, v.tensor_tensor(out_bind[:], sgn[:], ab[:], op=ALU.mult))
                    if last is True:
                        vstep(v, v.scalar_tensor_tensor(g_all[:], bndq[:], 0.125,
                                                        bndk[:], op0=ALU.mult,
                                                        op1=ALU.mult))

            @block.tensor
            def _(t):
                t.wait_ge(dve.h, dve.n)
                pe.inc(t.matmul(p0b[0:HC, 0:128], bndv[:], ident[:],
                                start=True, stop=True), 1)

            @block.scalar
            def _(s):
                s.wait_ge(pe.h, pe.n)
                act.inc(s.activation(vbtmp[:], p0b[0:HC, 0:128], AF.Copy), 1)

            @block.gpsimd
            def _(gp):
                gp.wait_ge(act.h, act.n)
                s_gW.inc(gp.dma_start(
                    out=vb_sc.rearrange("o (c p) -> (o c) p", p=128), in_=vbtmp[:]), 16)

        with contextlib.ExitStack() as P1:
            xbuf = [sb(P1, f"xbuf{i}", [128, D], F32) for i in range(4)]
            xhb = [sb(P1, f"xhb{i}", [128, D], BF16) for i in range(2)]
            gthb = [sb(P1, f"gthb{i}", [128, D], BF16) for i in range(2)]
            idxt = P1_IDXT[0]
            st6 = [sb(P1, f"st6_{i}", [128, 6 * (D // 512)], F32) for i in range(2)]
            mv4 = [sb(P1, f"mv4_{i}", [128, 8], F32) for i in range(2)]
            nmn4 = [sb(P1, f"nmn4_{i}", [128, 4], F32) for i in range(2)]
            lnv4 = [sb(P1, f"lnv4_{i}", [128, 4], F32) for i in range(2)]
            rsd4 = [sb(P1, f"rsd4_{i}", [128, 4], F32) for i in range(2)]
            nmr4 = [sb(P1, f"nmr4_{i}", [128, 4], F32) for i in range(2)]
            NCH = D // 512
            assert NSEQ % 4 == 0
            NGR = NSEQ // 4

            d1 = dve.n
            a1 = act.n
            x0 = [s.n for s in s_x]
            w0 = [s_xwp[0].n, s_xwp[1].n]
            msk0 = s_msk.n

            def BNE(i):
                return d1 + 10 * (i // 4) + 2 * (i % 4) + 1

            def NORME(i):
                return a1 + 6 * (i // 4) + 3 + (i % 4)

            def xload(eng, i):
                if i >= 4:
                    eng.wait_ge(dve.h, BNE(i - 4))
                    eng.wait_ge(act.h, NORME(i - 4))
                bix = i % 4
                s_x[bix].inc(eng.dma_start(out=xbuf[bix][:],
                                           in_=xs[i * 128:(i + 1) * 128, :]), 16)

            @block.sync
            def _(sy):
                pe.wait(sy)
                act.wait(sy)
                dve.wait(sy)
                s_gW.wait(sy)
                for i in range(NSEQ):
                    xload(sy, i)

            @block.gpsimd
            def _(gp):
                pe.wait(gp)
                act.wait(gp)
                dve.wait(gp)
                s_gW.wait(gp)
                gp.wait_ge(s_msk.h, msk0)
                from concourse.bass import IndirectOffsetOnAxis

                for j in range(NJ):
                    t_need = M + j
                    gp.wait_ge(s_xwp[0].h, w0[0] + 16 * (t_need // 2 + 1))
                    gp.wait_ge(s_xwp[1].h, w0[1] + 16 * ((t_need + 1) // 2))
                    if j >= 2:
                        gp.wait_ge(s_ab[j % 2].h, 16 * (j // 2))
                    s_gt[j % 2].inc(gp.indirect_dma_start(
                        out=gthb[j % 2][:], out_offset=None,
                        in_=xh_sc[0:(M + j + 1) * 128, :],
                        in_offset=IndirectOffsetOnAxis(ap=idxt[:, j:j + 1], axis=0)), 16)

            @block.vector
            def _(v):
                for g in range(NGR):
                    for t in range(4):
                        i = 4 * g + t
                        bix = i % 4
                        v.wait_ge(s_x[bix].h, x0[bix] + 16 * (i // 4 + 1))
                        if g >= 1 and t == 0:
                            v.wait_ge(act.h, a1 + 6 * (g - 1) + 2)
                        for ch in range(NCH):
                            ins = v.bn_stats(st6[i % 2][:, ch * 6:(ch + 1) * 6],
                                             xbuf[bix][:, ch * 512:(ch + 1) * 512])
                        dve.inc(ins, 1)
                        v.wait_ge(dve.h, d1 + 10 * g + 2 * t + 1)
                        dve.inc(v.bn_aggr(mv4[g % 2][:, 2 * t:2 * t + 2],
                                          st6[i % 2][:]), 1)
                    v.wait_ge(dve.h, dve.n)
                    mvv = mv4[g % 2][:].rearrange("p (t two) -> p t two", two=2)
                    dve.inc(v.tensor_scalar(nmn4[g % 2][:], mvv[:, :, 0], -1.0,
                                            None, op0=ALU.mult), 1)
                    v.wait_ge(dve.h, dve.n)
                    v.wait_ge(act.h, a1 + 6 * g + 2)
                    dve.inc(v.tensor_tensor(nmr4[g % 2][:], nmn4[g % 2][:],
                                            rsd4[g % 2][:], op=ALU.mult), 1)

            @block.scalar
            def _(s):
                pe.wait(s)
                s_gW.wait(s)

                def bounce(j):
                    s.wait_ge(s_gt[j % 2].h, 16 * (j // 2 + 1))
                    s_ab[j % 2].inc(s.dma_start(
                        out=xq_sc[j * 128:(j + 1) * 128, :], in_=gthb[j % 2][:]), 16)

                for g in range(NGR):
                    mvv = mv4[g % 2][:].rearrange("p (t two) -> p t two", two=2)
                    s.wait_ge(dve.h, d1 + 10 * g + 8)
                    act.inc(s.activation(lnv4[g % 2][:], mvv[:, :, 1], AF.Ln,
                                         bias=epsv[:], scale=1.0), 1)
                    s.wait_ge(act.h, act.n)
                    act.inc(s.activation(rsd4[g % 2][:], lnv4[g % 2][:], AF.Exp,
                                         scale=-0.5), 1)
                    s.wait_ge(dve.h, d1 + 10 * g + 10)
                    for t in range(4):
                        i = 4 * g + t
                        p = i % 2
                        s.wait_ge(s_xwp[p].h, w0[p] + 16 * (i // 2))
                        act.inc(s.activation(xhb[p][:], xbuf[i % 4][:], AF.Identity,
                                             bias=nmr4[g % 2][:, t:t + 1],
                                             scale=rsd4[g % 2][:, t:t + 1]), 1)
                        s.wait_ge(act.h, act.n)
                        s_xwp[p].inc(s.dma_start(
                            out=xh_sc[i * 128:(i + 1) * 128, :], in_=xhb[p][:]), 16)
                    for j in range(max(0, 4 * (g - 2) - M), 4 * (g - 1) - M):
                        if 0 <= j < NJ:
                            bounce(j)
                for j in range(max(0, 4 * (NGR - 2) - M), NJ):
                    bounce(j)

            assert dve.n == d1 + 10 * NGR
            assert act.n == a1 + 6 * NGR

        with contextlib.ExitStack() as P2:
            maskt = sb(P2, "maskt", [128, NMASK * PSA_W], BF16)
            vbflat = sb(P2, "vbflat", [1, D], BF16)
            vb_full = sb(P2, "vb_full", [128, D], BF16)
            Kh = sb(P2, "Kh", [128, CPH * T], BF16)
            Vh = sb(P2, "Vh", [128, NSEQ * HD], BF16)
            attnT = sb(P2, "attnT", [128, NKmax * PSA_W], BF16)
            qgb = sb(P2, "qgb", [128, CPH * PSA_W], BF16)
            astg = [sb(P2, f"astg{i}", [128, HD], BF16) for i in range(4)]
            pOut = [ps(P2, f"pOut{i}", [128, max(1024, HD)]) for i in range(4)]

            pe2 = pe.n
            act2 = act.n
            dve2 = dve.n
            peVB = pe.n
            actVB = act.n
            qg0 = s_qg.n
            xw_all = [s_xwp[0].n, s_xwp[1].n]
            kt00 = s_kt0.n
            ktR0 = s_ktR.n
            aw0 = [s.n for s in s_aw]

            NVB = D // 512

            def peU(u):
                return pe2 + NVB + sum(NK(uu % NGRP) + 1 for uu in range(u))

            def actU(u):
                return act2 + NVB + sum(NK(uu % NGRP) + 4 for uu in range(u))

            def PASSA(u, k):
                return peU(u) + k + 1

            def PASSB(u):
                return peU(u) + NK(u % NGRP) + 1

            def SIG(u, k):
                return actU(u) + k + 1

            def EVAC(u, tt):
                return actU(u) + NK(u % NGRP) + tt + 1

            @block.sync
            def _(sy):
                sy.wait_ge(act.h, act2)
                sy.wait_ge(dve.h, dve2)
                sy.wait_ge(pe.h, pe2)
                s_gW.wait(sy)
                sy.wait_ge(s_xwp[0].h, xw_all[0])
                sy.wait_ge(s_xwp[1].h, xw_all[1])
                sy.wait_ge(s_ab[0].h, s_ab[0].n)
                sy.wait_ge(s_ab[1].h, s_ab[1].n)
                sy.wait_ge(s_gt[0].h, s_gt[0].n)
                sy.wait_ge(s_gt[1].h, s_gt[1].n)
                s_vbf.inc(sy.dma_start(out=vbflat[:], in_=vb_sc), 16)
                s_msk.inc(sy.dma_start(
                    out=maskt[:].rearrange("p (m t) -> p m t", m=NMASK),
                    in_=masks_in.rearrange("m p t -> p m t")), 16)
                for u in range(NU):
                    h, g = u // NGRP, u % NGRP
                    if u > 0:
                        sy.wait_ge(pe.h, PASSA(u - 1, NK((u - 1) % NGRP) - 1))
                    if g == 0 and h == 0:
                        sy.wait_ge(s_xwp[0].h, xw_all[0])
                        sy.wait_ge(s_xwp[1].h, xw_all[1])
                    for c in range(CPH):
                        s_qg.inc(sy.dma_start(
                            out=qgb[:, c * PSA_W:(c + 1) * PSA_W],
                            in_=xq_sc[g * PSA_W:(g + 1) * PSA_W,
                                      (h * CPH + c) * 128:(h * CPH + c + 1) * 128],
                            transpose=True), 16)
                    if g == 0:
                        for c in range(CPH):
                            s_kt0.inc(sy.dma_start(
                                out=Kh[:, c * T: c * T + 512],
                                in_=xh_sc[0:512,
                                          (h * CPH + c) * 128:(h * CPH + c + 1) * 128],
                                transpose=True), 16)
                        for c in range(CPH):
                            s_ktR.inc(sy.dma_start(
                                out=Kh[:, c * T + 512: (c + 1) * T],
                                in_=xh_sc[512:T,
                                          (h * CPH + c) * 128:(h * CPH + c + 1) * 128],
                                transpose=True), 16)

            qgdone = {}
            vbdone = {}
            VSTAGES = []
            for si, (kb0, kb1) in enumerate([(0, 4), (4, min(16, NSEQ)), (16, NSEQ)]):
                if kb1 > kb0:
                    VSTAGES.append((kb0, kb1, [s_vt0, s_vtM, s_vtR][si]))
            vst0 = {id(sem): sem.n for (_, _, sem) in VSTAGES}

            @block.vector
            def _(v):
                v.wait_ge(act.h, actVB + NVB)
                for u in range(NU):
                    h, g = u // NGRP, u % NGRP
                    v.wait_ge(s_qg.h, qg0 + 16 * CPH * (u + 1))
                    for c in range(CPH):
                        ins = v.tensor_scalar(qgb[:, c * PSA_W:(c + 1) * PSA_W],
                                              qgb[:, c * PSA_W:(c + 1) * PSA_W],
                                              g_all[:, h * CPH + c:h * CPH + c + 1],
                                              None, op0=ALU.mult)
                    dve.inc(ins, 1)
                    qgdone[u] = dve.n
                    if g == 0:
                        for si, (kb0, kb1, sem) in enumerate(VSTAGES):
                            v.wait_ge(sem.h, vst0[id(sem)] + 16 * (h + 1))
                            for kt in range(kb0, kb1):
                                off = kt * HD
                                ins = v.tensor_tensor(
                                    Vh[:, off:off + HD], Vh[:, off:off + HD],
                                    vb_full[:, h * HD:(h + 1) * HD], op=ALU.mult)
                            dve.inc(ins, 1)
                            vbdone[(h, kb0)] = dve.n

            @block.tensor
            def _(t):
                s_msk.wait(t)
                s_vbf.wait(t)
                for o in range(NVB):
                    if o >= 2:
                        t.wait_ge(act.h, actVB + (o - 2) + 1)
                    pe.inc(t.matmul(pOut[o % 2][:, 0:512], onesrow[:],
                                    vbflat[:, o * 512:(o + 1) * 512],
                                    start=True, stop=True), 1)
                for u in range(NU):
                    h, g = u // NGRP, u % NGRP
                    nk = NK(g)
                    t.wait_ge(dve.h, qgdone[u])
                    for k in range(nk):
                        midx = k - GS * g
                        compact = GS <= midx < M
                        if k == 0:
                            t.wait_ge(s_kt0.h, kt00 + 16 * CPH * (h + 1))
                        elif k == 4:
                            t.wait_ge(s_ktR.h, ktR0 + 16 * CPH * (h + 1))
                        if k < 4:
                            if u > 0:
                                t.wait_ge(act.h, EVAC(u - 1, 0 if k < 2 else 1))
                            else:
                                t.wait_ge(act.h, actVB + NVB)
                        else:
                            t.wait_ge(act.h, SIG(u, k - 4))
                        has_mask = 0 <= midx < GS or M <= midx < M + GS
                        mdix = midx if midx < GS else GS + (midx - M)
                        if compact:
                            psA = pOut[(k % 4) // 2][:, (k % 2) * PSA_W:(k % 2) * PSA_W + 256]
                            for c in range(CPH):
                                qgu = qgb[:, c * PSA_W:(c + 1) * PSA_W].rearrange(
                                    "p (j q) -> p j q", q=128)[:, :, 64:128]
                                ins = t.matmul(psA,
                                               Kh[:, c * T + k * 128: c * T + (k + 1) * 128],
                                               qgu, start=(c == 0), stop=(c == CPH - 1))
                        else:
                            psA = pOut[(k % 4) // 2][:, (k % 2) * PSA_W:(k % 2 + 1) * PSA_W]
                            for c in range(CPH):
                                ins = t.matmul(psA,
                                               Kh[:, c * T + k * 128: c * T + (k + 1) * 128],
                                               qgb[:, c * PSA_W:(c + 1) * PSA_W],
                                               start=(c == 0),
                                               stop=(c == CPH - 1 and not has_mask))
                            if has_mask:
                                ins = t.matmul(psA, ident[:],
                                               maskt[:, mdix * PSA_W:(mdix + 1) * PSA_W],
                                               start=False, stop=True)
                        pe.inc(ins, 1)
                    t.wait_ge(act.h, SIG(u, nk - 1))
                    for k in range(nk):
                        midx = k - GS * g
                        compact = GS <= midx < M
                        if (h, k) in vbdone:
                            t.wait_ge(dve.h, vbdone[(h, k)])
                        for tt in range(GS):
                            for dh in range(NDH):
                                if compact:
                                    ins = t.matmul(
                                        pOut[tt][64:128, dh * DHW:(dh + 1) * DHW],
                                        attnT[:, k * PSA_W + tt * 64: k * PSA_W + (tt + 1) * 64],
                                        Vh[:, k * HD + dh * DHW: k * HD + (dh + 1) * DHW],
                                        start=False, stop=(k == nk - 1))
                                else:
                                    ins = t.matmul(
                                        pOut[tt][:, dh * DHW:(dh + 1) * DHW],
                                        attnT[:, k * PSA_W + tt * 128: k * PSA_W + (tt + 1) * 128],
                                        Vh[:, k * HD + dh * DHW: k * HD + (dh + 1) * DHW],
                                        start=(k == 0), stop=(k == nk - 1))
                    pe.inc(ins, 1)

            aw_after = {}

            @block.scalar
            def _(s):
                for o in range(NVB):
                    s.wait_ge(pe.h, peVB + o + 1)
                    act.inc(s.activation(vb_full[:, o * 512:(o + 1) * 512],
                                         pOut[o % 2][:, 0:512], AF.Copy), 1)
                for u in range(NU):
                    h, g = u // NGRP, u % NGRP
                    nk = NK(g)
                    if g == 0:
                        if h > 0:
                            s.wait_ge(pe.h, peU(u))
                        else:
                            s.wait_ge(s_xwp[0].h, xw_all[0])
                            s.wait_ge(s_xwp[1].h, xw_all[1])
                        for (kb0, kb1, sem) in VSTAGES:
                            sem.inc(s.dma_start(
                                out=Vh[:, kb0 * HD:kb1 * HD].rearrange(
                                    "p (k d) -> p k d", k=kb1 - kb0),
                                in_=xh_sc[kb0 * 128:kb1 * 128,
                                          h * HD:(h + 1) * HD].rearrange(
                                    "(k p) d -> p k d", p=128)), 16)
                    for k in range(nk):
                        midx = k - GS * g
                        w = 256 if GS <= midx < M else PSA_W
                        s.wait_ge(pe.h, PASSA(u, k))
                        act.inc(s.activation(
                            attnT[:, k * PSA_W:k * PSA_W + w],
                            pOut[(k % 4) // 2][:, (k % 2) * PSA_W:(k % 2) * PSA_W + w],
                            AF.Sigmoid), 1)
                    s.wait_ge(pe.h, PASSB(u))
                    if u > 0:
                        gp_prev = (u - 1) % NGRP
                        s.wait_ge(s_aw[gp_prev].h, aw_after[u - 1])
                    for tt in range(GS):
                        act.inc(s.activation(astg[tt][:], pOut[tt][:, 0:HD], AF.Copy), 1)
                        s.wait_ge(act.h, act.n)
                        row = g * PSA_W + tt * 128
                        s_aw[g].inc(s.dma_start(
                            out=a_sc[row:row + 128, h * HD:(h + 1) * HD],
                            in_=astg[tt][:]), 16)
                    aw_after[u] = s_aw[g].n

            assert pe.n == peU(NU), (pe.n, peU(NU))
            assert act.n == actU(NU), (act.n, actU(NU))

        with contextlib.ExitStack() as P3S:
            yfm = sb(P3S, "yfm", [128, HC * TH], BF16)
            hgTa = sb(P3S, "hgTa", [112, TH], BF16)
            hgTb = sb(P3S, "hgTb", [112, TH], BF16)

            with contextlib.ExitStack() as PP:
                ybuf = [sb(PP, f"ybuf{i}", [128, D], F32) for i in range(2)]
                yhb = [sb(PP, f"yhb{i}", [128, D], BF16) for i in range(2)]
                abuf = [sb(PP, f"abuf{i}", [128, D], BF16) for i in range(2)]
                st6p = [sb(PP, f"st6p{i}", [128, 6 * (D // 512)], F32) for i in range(2)]
                mvp = [sb(PP, f"mvp{i}", [128, 2], F32) for i in range(2)]
                nmnp = [sb(PP, f"nmnp{i}", [128, 1], F32) for i in range(2)]
                sqvp = [sb(PP, f"sqvp{i}", [128, 1], F32) for i in range(2)]
                rsdp = [sb(PP, f"rsdp{i}", [128, 1], F32) for i in range(2)]
                nmrp = [sb(PP, f"nmrp{i}", [128, 1], F32) for i in range(2)]
                pq = [ps(PP, f"pq{i}", [128, 512]) for i in range(8)]
                NCH = D // 512
                DSP = max(512, min(D - 512, (D * 5 // 8) // 512 * 512))

                pe3 = pe.n
                act3 = act.n
                dve3 = dve.n
                pool3 = pool.n
                yw0v = [s_yw[0].n, s_yw[1].n]
                awF = [s.n for s in s_aw]
                yb0v = [s.n for s in s_yb]
                ac0v = [s.n for s in s_ac]

                DVE_PJ, ACT_PJ, PE_PJ = 3, 13, 2

                def ADDS(j):
                    return dve3 + DVE_PJ * j + 1

                def AGGRP(j):
                    return dve3 + DVE_PJ * j + 3

                def NORMP(j):
                    return act3 + ACT_PJ * j + 5

                def EVA(j, r, q):
                    return act3 + ACT_PJ * j + 5 + r * 4 + q + 1

                def TRND(j, r):
                    return pe3 + PE_PJ * j + r + 1

                def ywcnt(j):
                    return yw0v[j % 2] + 16 * (j // 2 + 1)

                @block.sync
                def _(sy):
                    sy.wait_ge(pe.h, pe3)
                    sy.wait_ge(act.h, act3)
                    sy.wait_ge(dve.h, dve3)
                    for g4 in range(4):
                        sy.wait_ge(s_aw[g4].h, awF[g4])
                    for j in range(NJ):
                        p = j % 2
                        if j >= 2:
                            sy.wait_ge(dve.h, AGGRP(j - 2))
                            sy.wait_ge(act.h, NORMP(j - 2))
                            sy.wait_ge(s_yw[p].h, ywcnt(j - 2))
                        s_yb[p].inc(sy.dma_start(
                            out=ybuf[p][:], in_=xq[j * 128:(j + 1) * 128, :]), 16)

                @block.gpsimd
                def _(gp):
                    gp.wait_ge(pe.h, pe3)
                    gp.wait_ge(act.h, act3)
                    for g4 in range(4):
                        gp.wait_ge(s_aw[g4].h, awF[g4])
                    def a_load(jj):
                        pj = jj % 2
                        if jj >= 2:
                            gp.wait_ge(dve.h, ADDS(jj - 2))
                            gp.wait_ge(pool.h, pool3 + jj - 1)
                        s_ac[pj].inc(gp.dma_start(
                            out=abuf[pj][:], in_=a_sc[jj * 128:(jj + 1) * 128, :]), 16)
                    a_load(0)
                    for j in range(NJ):
                        p = j % 2
                        if j + 1 < NJ:
                            a_load(j + 1)
                        gp.wait_ge(s_yb[p].h, yb0v[p] + 16 * (j // 2 + 1))
                        gp.wait_ge(s_ac[p].h, ac0v[p] + 16 * (j // 2 + 1))
                        pool.inc(gp.tensor_tensor(ybuf[p][:, DSP:D],
                                                  ybuf[p][:, DSP:D],
                                                  abuf[p][:, DSP:D], op=ALU.add), 1)

                @block.vector
                def _(v):
                    for j in range(NJ):
                        p = j % 2
                        v.wait_ge(s_yb[p].h, yb0v[p] + 16 * (j // 2 + 1))
                        v.wait_ge(s_ac[p].h, ac0v[p] + 16 * (j // 2 + 1))
                        dve.inc(v.tensor_tensor(ybuf[p][:, 0:DSP], ybuf[p][:, 0:DSP],
                                                abuf[p][:, 0:DSP], op=ALU.add), 1)
                        v.wait_ge(dve.h, ADDS(j))
                        v.wait_ge(pool.h, pool3 + j + 1)
                        for ch in range(NCH):
                            ins = v.bn_stats(st6p[j % 2][:, ch * 6:(ch + 1) * 6],
                                             ybuf[p][:, ch * 512:(ch + 1) * 512])
                        dve.inc(ins, 1)
                        v.wait_ge(dve.h, dve.n)
                        if j >= 2:
                            v.wait_ge(act.h, act3 + ACT_PJ * (j - 2) + 2)
                        dve.inc(v.bn_aggr(mvp[j % 2][:], st6p[j % 2][:]), 1)

                @block.scalar
                def _(s):
                    for j in range(NJ):
                        p = j % 2
                        s.wait_ge(dve.h, AGGRP(j))
                        act.inc(s.activation(nmnp[p][:], mvp[p][:, 0:1], AF.Identity,
                                             scale=-1.0), 1)
                        act.inc(s.activation(sqvp[p][:], mvp[p][:, 1:2], AF.Ln,
                                             bias=epsv[:], scale=1.0), 1)
                        s.wait_ge(act.h, act.n)
                        act.inc(s.activation(rsdp[p][:], sqvp[p][:], AF.Exp,
                                             scale=-0.5), 1)
                        s.wait_ge(act.h, act.n)
                        act.inc(s.activation(nmrp[p][:], nmnp[p][:], AF.Identity,
                                             scale=rsdp[p][:]), 1)
                        s.wait_ge(act.h, act.n)
                        if j >= 2:
                            s.wait_ge(pe.h, TRND(j - 2, 1))
                        act.inc(s.activation(yhb[j % 2][:], ybuf[p][:], AF.Identity,
                                             bias=nmrp[p][:], scale=rsdp[p][:]), 1)
                        s.wait_ge(act.h, act.n)
                        s.wait_ge(s_yw[p].h, yw0v[p] + 16 * (j // 2))
                        s_yw[p].inc(s.dma_start(out=out_hbm[j * 128:(j + 1) * 128, :],
                                                in_=ybuf[p][:]), 16)
                        for r in range(2):
                            s.wait_ge(pe.h, TRND(j, r))
                            for q in range(4):
                                cbase = r * (HC // 2) + q * CPB
                                act.inc(s.activation(
                                    yfm[:].rearrange("p (c t) -> p c t", t=TH)[
                                        :, cbase:cbase + CPB, j * 128:(j + 1) * 128],
                                    pq[r * 4 + q][:, 0:CPB * 128].rearrange(
                                        "p (c t) -> p c t", t=128),
                                    AF.Copy), 1)

                @block.tensor
                def _(t):
                    for j in range(NJ):
                        t.wait_ge(act.h, NORMP(j))
                        for r in range(2):
                            if j >= 1:
                                t.wait_ge(act.h, EVA(j - 1, r, 3))
                            for ci in range(HC // 2):
                                c = r * (HC // 2) + ci
                                q, col = ci // CPB, ci % CPB
                                ins = t.matmul(
                                    pq[r * 4 + q][:, col * 128:(col + 1) * 128],
                                    yhb[j % 2][:, c * 128:(c + 1) * 128],
                                    ident[:], start=True, stop=True)
                            pe.inc(ins, 1)

                assert dve.n == dve3 + DVE_PJ * NJ
                assert act.n == act3 + ACT_PJ * NJ
                assert pe.n == pe3 + PE_PJ * NJ

            with contextlib.ExitStack() as PG:
                RANK = cfg["RANK"]
                agb = sb(PG, "agb", [128, HC * NARE], BF16)
                lgt_all = sb(PG, "lgt_all", [128, NJ * NARE], F32)
                m1a = sb(PG, "m1a", [128, NJ], F32)
                m2a = sb(PG, "m2a", [128, NJ], F32)
                d12a = sb(PG, "d12a", [128, NJ], F32)
                p1a = sb(PG, "p1a", [128, NJ], F32)
                p2a = sb(PG, "p2a", [128, NJ], F32)
                km1a = sb(PG, "km1a", [128, NJ * NE], F32)
                km2a = sb(PG, "km2a", [128, NJ * NE], F32)
                l2a = sb(PG, "l2a", [128, NJ * NE], F32)
                g28a = sb(PG, "g28a", [128, NJ * NE], F32)
                hgball = sb(PG, "hgball", [128, NJ * NER], BF16)
                psL = [ps(PG, "psL0", [128, 512]), ps(PG, "psL1", [128, 512])]
                psT = [ps(PG, "psT0", [128, 512]), ps(PG, "psT1", [128, 512])]

                peG = pe.n
                actG = act.n
                dveG = dve.n
                gw0 = s_gW.n

                def LMM(j):
                    return peG + j + 1

                def TMM(j):
                    return peG + NJ + j + 1

                def LG(j):
                    return dveG + j + 1

                D_L = dveG + NJ
                D_M1 = D_L + 1
                D_KM1 = D_M1 + NJ
                D_L2 = D_KM1 + NJ
                D_M2 = D_L2 + 1
                D_KM2 = D_M2 + NJ
                D_D12 = D_KM2 + 1
                D_G28a = D_D12 + NJ
                D_G28b = D_G28a + NJ
                D_SC = D_G28b + 1
                D_HG = D_SC + RANK
                A_P = actG + 2

                def A_EV(j):
                    return A_P + 2 * (j + 1)

                @block.gpsimd
                def _(gp):
                    gp.wait_ge(act.h, actG)
                    gp.wait_ge(dve.h, dveG)
                    gp.wait_ge(pe.h, peG)
                    gp.wait_ge(s_yw[0].h, s_yw[0].n)
                    gp.wait_ge(s_yw[1].h, s_yw[1].n)
                    for sy3 in s_yb + s_ac:
                        gp.wait_ge(sy3.h, sy3.n)
                    s_gW.inc(gp.dma_start(
                        out=agb[:].rearrange("p (c e) -> p c e", e=NARE),
                        in_=ag_in.rearrange("(c p) e -> p c e", p=128)), 16)

                @block.tensor
                def _(t):
                    t.wait_ge(s_gW.h, gw0 + 16)
                    t.wait_ge(act.h, actG)
                    for j in range(NJ):
                        if j >= 2:
                            t.wait_ge(dve.h, LG(j - 2))
                        for c in range(HC):
                            ins = t.matmul(psL[j % 2][:, 0:NARE],
                                           yfm[:, c * TH + j * 128: c * TH + (j + 1) * 128],
                                           agb[:, c * NARE:(c + 1) * NARE],
                                           start=(c == 0), stop=(c == HC - 1))
                        pe.inc(ins, 1)
                    for j in range(NJ):
                        t.wait_ge(dve.h, D_HG)
                        if j >= 2:
                            t.wait_ge(act.h, A_EV(j - 2))
                        t.matmul(psT[j % 2][0:112, 0:128],
                                 hgball[:, j * NER:j * NER + 112], ident[:],
                                 start=True, stop=True)
                        pe.inc(t.matmul(psT[j % 2][0:112, 128:256],
                                        hgball[:, j * NER + 112:(j + 1) * NER], ident[:],
                                        start=True, stop=True), 1)

                @block.vector
                def _(v):
                    lga = lgt_all[:].rearrange("p (j e) -> p j e", e=NARE)
                    for j in range(NJ):
                        v.wait_ge(pe.h, LMM(j))
                        dve.inc(v.tensor_copy(lgt_all[:, j * NARE:(j + 1) * NARE],
                                              psL[j % 2][:, 0:NARE]), 1)
                    v.wait_ge(dve.h, D_L)
                    dve.inc(v.reduce_max(m1a[:], lga[:, :, 0:NE], axis=AXX), 1)
                    v.wait_ge(dve.h, D_M1)
                    for j in range(NJ):
                        dve.inc(v.tensor_scalar(km1a[:, j * NE:(j + 1) * NE],
                                                lgt_all[:, j * NARE:j * NARE + NE],
                                                m1a[:, j:j + 1], None, op0=ALU.is_ge), 1)
                    v.wait_ge(dve.h, D_KM1)
                    for j in range(NJ):
                        dve.inc(v.scalar_tensor_tensor(
                            l2a[:, j * NE:(j + 1) * NE], km1a[:, j * NE:(j + 1) * NE],
                            -BIGL, lgt_all[:, j * NARE:j * NARE + NE],
                            op0=ALU.mult, op1=ALU.add), 1)
                    v.wait_ge(dve.h, D_L2)
                    dve.inc(v.reduce_max(m2a[:], l2a[:].rearrange(
                        "p (j e) -> p j e", e=NE), axis=AXX), 1)
                    v.wait_ge(dve.h, D_M2)
                    for j in range(NJ):
                        dve.inc(v.tensor_scalar(km2a[:, j * NE:(j + 1) * NE],
                                                l2a[:, j * NE:(j + 1) * NE],
                                                m2a[:, j:j + 1], None, op0=ALU.is_ge), 1)
                    v.wait_ge(dve.h, D_KM2)
                    dve.inc(v.tensor_tensor(d12a[:], m1a[:], m2a[:], op=ALU.subtract), 1)
                    v.wait_ge(act.h, A_P)
                    for j in range(NJ):
                        dve.inc(v.tensor_scalar(g28a[:, j * NE:(j + 1) * NE],
                                                km1a[:, j * NE:(j + 1) * NE],
                                                p1a[:, j:j + 1], None, op0=ALU.mult), 1)
                    v.wait_ge(dve.h, D_G28a)
                    for j in range(NJ):
                        dve.inc(v.scalar_tensor_tensor(
                            g28a[:, j * NE:(j + 1) * NE], km2a[:, j * NE:(j + 1) * NE],
                            p2a[:, j:j + 1], g28a[:, j * NE:(j + 1) * NE],
                            op0=ALU.mult, op1=ALU.add), 1)
                    v.wait_ge(dve.h, D_G28b)
                    dve.inc(v.tensor_scalar(g28a[:], g28a[:], 2.0, None, op0=ALU.mult), 1)
                    v.wait_ge(dve.h, D_SC)
                    hgv = hgball[:].rearrange("p (j e r) -> p j e r", e=NE, r=RANK)
                    g28v = g28a[:].rearrange("p (j e) -> p j e", e=NE)
                    lgh = lga[:, :, NE:NARE].rearrange("p j (e r) -> p j e r", r=RANK)
                    for r8 in range(RANK):
                        dve.inc(v.tensor_tensor(
                            hgv[:, :, :, r8], lgh[:, :, :, r8],
                            g28v[:, :, :], op=ALU.mult), 1)

                @block.scalar
                def _(s):
                    s.wait_ge(dve.h, D_D12)
                    act.inc(s.activation(p1a[:], d12a[:], AF.Sigmoid, scale=1.0), 1)
                    act.inc(s.activation(p2a[:], d12a[:], AF.Sigmoid, scale=-1.0), 1)
                    for j in range(NJ):
                        s.wait_ge(pe.h, TMM(j))
                        act.inc(s.activation(hgTa[:, j * 128:(j + 1) * 128],
                                             psT[j % 2][0:112, 0:128], AF.Copy), 1)
                        act.inc(s.activation(hgTb[:, j * 128:(j + 1) * 128],
                                             psT[j % 2][0:112, 128:256], AF.Copy), 1)

                assert dve.n == D_HG
                assert act.n == A_P + 2 * NJ
                assert pe.n == peG + 2 * NJ

            with contextlib.ExitStack() as PM:
                wtb = [sb(PM, f"wtb{i}", [128, HC * OGW], BF16) for i in range(2)]
                b2ab = sb(PM, "b2ab", [112, D], BF16)
                b2bb = sb(PM, "b2bb", [112, D], BF16)
                ypc = [sb(PM, f"ypc{i}", [128, OGW], F32) for i in range(2)]
                ost = [sb(PM, f"ost{i}", [128, OGW], F32) for i in range(2)]
                pm = [ps(PM, f"pm{i}", [128, 512]) for i in range(4)]

                peM = pe.n
                dveM = dve.n
                actM = act.n
                gwM = s_gW.n
                yp0v = [s_yp[0].n, s_yp[1].n]
                os0v = [s_os[0].n, s_os[1].n]
                wt0v = [s_wt[0].n, s_wt[1].n]
                NID = NOG * NJ

                def MME(i):
                    return peM + i + 1

                def EVE(i):
                    return dveM + i + 1

                @block.gpsimd
                def _(gp):
                    gp.wait_ge(act.h, actM)
                    gp.wait_ge(dve.h, dveM)
                    gp.wait_ge(pe.h, peM)
                    s_gW.inc(gp.dma_start(out=b2ab[:], in_=b2a_in), 16)
                    s_gW.inc(gp.dma_start(out=b2bb[:], in_=b2b_in), 16)
                    for og in range(NOG):
                        if og >= 2:
                            gp.wait_ge(pe.h, MME((og - 1) * NJ - 1))
                        for q in range(4):
                            s_wt[og % 2].inc(gp.dma_start(
                                out=wtb[og % 2][:, q * HCG * OGW:(q + 1) * HCG * OGW
                                                ].rearrange("p (c o) -> p c o", o=OGW),
                                in_=wt_in[q * HCG * 128:(q + 1) * HCG * 128,
                                          og * OGW:(og + 1) * OGW
                                          ].rearrange("(c p) o -> p c o", p=128)), 16)

                @block.tensor
                def _(t):
                    t.wait_ge(s_gW.h, gwM + 32)
                    t.wait_ge(act.h, actM)
                    t.wait_ge(dve.h, dveM)
                    for og in range(NOG):
                        t.wait_ge(s_wt[og % 2].h, wt0v[og % 2] + 64 * (og // 2 + 1))
                        for tt in range(NJ):
                            i = og * NJ + tt
                            if i >= 4:
                                t.wait_ge(dve.h, EVE(i - 4))
                            bank = pm[tt % 4]
                            for c in range(HC):
                                t.matmul(bank[:, 0:OGW],
                                         yfm[:, c * TH + tt * 128: c * TH + (tt + 1) * 128],
                                         wtb[og % 2][:, c * OGW:(c + 1) * OGW],
                                         start=(c == 0), stop=False)
                            t.matmul(bank[:, 0:OGW],
                                     hgTa[:, tt * 128:(tt + 1) * 128],
                                     b2ab[:, og * OGW:(og + 1) * OGW],
                                     start=False, stop=False)
                            pe.inc(t.matmul(bank[:, 0:OGW],
                                            hgTb[:, tt * 128:(tt + 1) * 128],
                                            b2bb[:, og * OGW:(og + 1) * OGW],
                                            start=False, stop=True), 1)

                @block.sync
                def _(sy):
                    sy.wait_ge(s_yw[0].h, s_yw[0].n)
                    sy.wait_ge(s_yw[1].h, s_yw[1].n)
                    sy.wait_ge(act.h, actM)
                    sy.wait_ge(dve.h, dveM)
                    sy.wait_ge(pe.h, peM)
                    for i in range(NID):
                        og, tt = i // NJ, i % NJ
                        p = i % 2
                        if i >= 2:
                            sy.wait_ge(dve.h, EVE(i - 2))
                        s_yp[p].inc(sy.dma_start(
                            out=ypc[p][:],
                            in_=out_hbm[tt * 128:(tt + 1) * 128,
                                        og * OGW:(og + 1) * OGW]), 16)

                @block.vector
                def _(v):
                    for i in range(NID):
                        og, tt = i // NJ, i % NJ
                        p = i % 2
                        v.wait_ge(pe.h, MME(i))
                        v.wait_ge(s_yp[p].h, yp0v[p] + 16 * (i // 2 + 1))
                        if i >= 2:
                            v.wait_ge(s_os[p].h, os0v[p] + 16 * (i // 2))
                        dve.inc(v.tensor_tensor(ost[p][:], pm[tt % 4][:, 0:OGW],
                                                ypc[p][:], op=ALU.add), 1)

                @block.scalar
                def _(s):
                    for i in range(NID):
                        og, tt = i // NJ, i % NJ
                        p = i % 2
                        s.wait_ge(dve.h, EVE(i))
                        s_os[p].inc(s.dma_start(
                            out=out_hbm[tt * 128:(tt + 1) * 128,
                                        og * OGW:(og + 1) * OGW],
                            in_=ost[p][:]), 16)

                @block.sync
                def _(sy):
                    sy.wait_ge(s_os[0].h, s_os[0].n)
                    sy.wait_ge(s_os[1].h, s_os[1].n)

    return nc


_PROG_CACHE = {}


def _get_program(key, cfg):
    if key not in _PROG_CACHE:
        _PROG_CACHE[key] = build_program(cfg)
    return _PROG_CACHE[key]


def run(cfg, x, bvq, bvk, bvv, W, gateW, loraA, loraB, core_ids=None, sim=False,
        sim_cores=2):
    in_maps, gmaps = stage_inputs(cfg, x, bvq, bvk, bvv, W, gateW, loraA, loraB)
    nc = _get_program((cfg["T"], cfg["D"]), cfg)
    if sim:
        from simrun import simulate
        results = simulate(nc, in_maps[:sim_cores], ["out"], num_cores=sim_cores)
    else:
        results = run_bass_kernel_spmd(
            nc, in_maps, core_ids=core_ids or list(range(8))).results
    B = x.shape[0]
    out = np.zeros((B, cfg["T"], cfg["D"]), dtype=np.float32)
    for c in range(len(results)):
        b, gmap = gmaps[c]
        out[b][gmap] = results[c]["out"]
    return out


def kernel(L0_attn_x, L0_attn_bv_q, L0_attn_bv_k, L0_attn_bv_v,
           L0_ffn_base_W, L0_ffn_gate_W, L0_ffn_lora_A, L0_ffn_lora_B):
    cfg = make_cfg(T=4096, D=4096)
    return run(cfg, np.asarray(L0_attn_x), np.asarray(L0_attn_bv_q),
               np.asarray(L0_attn_bv_k), np.asarray(L0_attn_bv_v),
               np.asarray(L0_ffn_base_W), np.asarray(L0_ffn_gate_W),
               np.asarray(L0_ffn_lora_A), np.asarray(L0_ffn_lora_B))


# version 18
# speedup vs baseline: 1.0194x; 1.0194x over previous
import contextlib
import numpy as np
import ml_dtypes
import concourse.bass as bass
import concourse.mybir as mybir
from concourse.bass_utils import run_bass_kernel_spmd

F32 = mybir.dt.float32
BF16 = mybir.dt.bfloat16
AF = mybir.ActivationFunctionType
ALU = mybir.AluOpType
AXX = mybir.AxisListType.X

NEG = -30000.0
BIGL = 1.0e9


def make_cfg(T=4096, D=4096):
    H = 4
    cfg = dict(T=T, D=D, H=H, HD=D // H)
    cfg["HC"] = D // 128
    cfg["CPH"] = cfg["HD"] // 128
    cfg["NSEQ"] = T // 128
    cfg["TH"] = T // 2
    cfg["NJ"] = cfg["TH"] // 128
    cfg["GS"] = 4
    assert cfg["NJ"] % cfg["GS"] == 0
    cfg["NGRP"] = cfg["NJ"] // cfg["GS"]
    cfg["M"] = cfg["TH"] // 128
    cfg["NMASK"] = 2 * cfg["GS"]
    cfg["NE"] = 28
    cfg["RANK"] = 8
    cfg["NER"] = 224
    cfg["NARE"] = 252
    cfg["OGW"] = 256
    cfg["NOG"] = D // cfg["OGW"]
    cfg["NDH"] = max(1, cfg["HD"] // 512)
    cfg["DHW"] = min(512, cfg["HD"])
    return cfg



def local_to_global_rows(cfg, parity):
    TH, NJ = cfg["TH"], cfg["NJ"]
    gmap = np.zeros(TH, dtype=np.int64)
    for j in range(NJ):
        for p in range(64):
            gmap[j * 128 + p] = 128 * j + 64 * parity + p
            gmap[j * 128 + 64 + p] = TH + 128 * j + 64 * parity + p
    return gmap


def build_masks(cfg, parity):
    GS, M = cfg["GS"], cfg["M"]
    out = np.zeros((cfg["NMASK"], 128, GS * 128), dtype=np.float32)
    s = np.arange(128)[:, None]
    srot = (s + 64 * parity) % 128
    p = np.arange(128)[None, :]
    lower = p < 64
    upper = ~lower
    for idx in range(cfg["NMASK"]):
        midx = idx if idx < GS else M + (idx - GS)
        for jj in range(GS):
            d = midx - jj
            if d < 0:
                continue
            blk = out[idx][:, jj * 128:(jj + 1) * 128]
            if d == 0:
                qrel = 64 * parity + p
                blk += np.where(lower & (srot > qrel), NEG, 0.0)
            elif d < M:
                blk += np.where(lower, NEG, 0.0)
            elif d == M:
                qrel = 64 * parity + (p - 64)
                blk += np.where(lower | (upper & (srot > qrel)), NEG, 0.0)
            else:
                blk += NEG
    return np.clip(out, NEG, 0.0).astype(ml_dtypes.bfloat16)


def bv_rlayout(bv):
    flat = np.ascontiguousarray(bv, dtype=np.float32).reshape(-1)
    return np.ascontiguousarray(flat.reshape(-1, 128).T)


def stage_inputs(cfg, x, bvq, bvk, bvv, W, gateW, loraA, loraB):
    D = cfg["D"]
    wt = np.ascontiguousarray(np.asarray(W).T, dtype=np.float32)
    A2 = np.ascontiguousarray(np.asarray(loraA).reshape(cfg["NE"] * cfg["RANK"], D))
    ag = np.ascontiguousarray(
        np.concatenate([np.asarray(gateW).T, A2.T], axis=1), dtype=np.float32)
    B2 = np.ascontiguousarray(
        np.transpose(np.asarray(loraB), (0, 2, 1)).reshape(cfg["NER"], D))
    bq, bk, bv_ = bv_rlayout(bvq), bv_rlayout(bvk), bv_rlayout(bvv)
    in_maps, gmaps = [], []
    for c in range(8):
        b, parity = c // 2, c & 1
        bb = min(b, x.shape[0] - 1)
        gmap = local_to_global_rows(cfg, parity)
        gmaps.append((bb, gmap))
        xr = np.asarray(x[bb], dtype=np.float32).reshape(-1, 128, D)
        xr = np.concatenate([xr[:, 64 * parity:, :], xr[:, :64 * parity, :]],
                            axis=1).reshape(-1, D)
        in_maps.append({
            "xs": np.ascontiguousarray(xr),
            "xq": np.ascontiguousarray(x[bb][gmap], dtype=np.float32),
            "masks": build_masks(cfg, parity),
            "bvq": bq, "bvk": bk, "bvv": bv_,
            "wt": wt, "ag": ag,
            "b2a": np.ascontiguousarray(B2[0:112], dtype=np.float32),
            "b2b": np.ascontiguousarray(B2[112:224], dtype=np.float32),
        })
    return in_maps, gmaps



class Sem:
    def __init__(self, handle):
        self.h = handle
        self.n = 0

    def inc(self, instr, k):
        instr.then_inc(self.h, k)
        self.n += k
        return instr

    def wait(self, eng, val=None):
        v = self.n if val is None else val
        if v > 0:
            eng.wait_ge(self.h, v)


def build_program(cfg):
    T, D, H = cfg["T"], cfg["D"], cfg["H"]
    HC, CPH, HD = cfg["HC"], cfg["CPH"], cfg["HD"]
    NSEQ, TH, NJ = cfg["NSEQ"], cfg["TH"], cfg["NJ"]
    GS, NGRP, M, NMASK = cfg["GS"], cfg["NGRP"], cfg["M"], cfg["NMASK"]
    NE, NER, NARE = cfg["NE"], cfg["NER"], cfg["NARE"]
    OGW, NOG = cfg["OGW"], cfg["NOG"]
    NDH, DHW = cfg["NDH"], cfg["DHW"]
    NTOT = NSEQ + NJ
    NTB = NSEQ // 4
    HCG = HC // 4
    CPB = max(1, HC // 8)
    PSA_W = GS * 128
    NKmax = M + GS * NGRP
    NU = H * NGRP

    def NK(g):
        return M + GS * (g + 1)

    nc = bass.Bass("TRN2", target_bir_lowering=False)

    xs = nc.dram_tensor("xs", [T, D], F32, kind="ExternalInput").ap()
    xq = nc.dram_tensor("xq", [TH, D], F32, kind="ExternalInput").ap()
    masks_in = nc.dram_tensor("masks", [NMASK, 128, PSA_W], BF16, kind="ExternalInput").ap()
    bvq_in = nc.dram_tensor("bvq", [128, HC], F32, kind="ExternalInput").ap()
    bvk_in = nc.dram_tensor("bvk", [128, HC], F32, kind="ExternalInput").ap()
    bvv_in = nc.dram_tensor("bvv", [128, HC], F32, kind="ExternalInput").ap()
    wt_in = nc.dram_tensor("wt", [D, D], F32, kind="ExternalInput").ap()
    ag_in = nc.dram_tensor("ag", [D, NARE], F32, kind="ExternalInput").ap()
    b2a_in = nc.dram_tensor("b2a", [112, D], F32, kind="ExternalInput").ap()
    b2b_in = nc.dram_tensor("b2b", [112, D], F32, kind="ExternalInput").ap()
    out_hbm = nc.dram_tensor("out", [TH, D], F32, kind="ExternalOutput").ap()

    xh_sc = nc.dram_tensor("xh_sc", [T, D], BF16, kind="Internal").ap()
    xq_sc = nc.dram_tensor("xq_sc", [TH, D], BF16, kind="Internal").ap()
    a_sc = nc.dram_tensor("a_sc", [TH, D], BF16, kind="Internal").ap()
    vb_sc = nc.dram_tensor("vb_sc", [1, D], BF16, kind="Internal").ap()

    with contextlib.ExitStack() as OS:
        def sb(stack, name, shp, dt=F32):
            return stack.enter_context(nc.sbuf_tensor(name, shp, dt))

        def ps(stack, name, shp):
            return stack.enter_context(nc.psum_tensor(name, shp, F32))

        ident = sb(OS, "ident", [128, 128], BF16)
        g_all = sb(OS, "g_all", [128, HC], F32)
        epsv = sb(OS, "epsv", [128, 1], F32)
        onesrow = sb(OS, "onesrow", [1, 128], BF16)

        sem_names = [
            "s_x0", "s_x1", "s_x2", "s_x3", "s_x4", "s_x5", "s_x6", "s_x7",
            "s_msk", "s_qg", "s_yp0", "s_yp1", "s_ab0", "s_ab1",
            "s_vbf", "s_xw", "s_xw1", "s_yb0", "s_yb1", "s_yb2",
            "s_aw0", "s_aw1", "s_aw2", "s_aw3", "s_yw0", "s_yw1",
            "s_os0", "s_os1", "s_gW", "s_wt0", "s_wt1",
            "s_pe", "s_dve", "s_act", "s_pool",
        ]
        SS = {nm: Sem(OS.enter_context(nc.semaphore(nm))) for nm in sem_names}
        s_vtM = Sem(OS.enter_context(nc.semaphore("s_vtM")))
        s_ac = [Sem(OS.enter_context(nc.semaphore(f"s_ac{i}"))) for i in range(3)]
        s_gt = [Sem(OS.enter_context(nc.semaphore(f"s_gt{i}"))) for i in range(2)]
        s_xqw = Sem(OS.enter_context(nc.semaphore("s_xqw")))
        s_kt0 = Sem(OS.enter_context(nc.semaphore("s_kt0")))
        s_ktR = Sem(OS.enter_context(nc.semaphore("s_ktR")))
        s_vt0 = Sem(OS.enter_context(nc.semaphore("s_vt0")))
        s_vtR = Sem(OS.enter_context(nc.semaphore("s_vtR")))
        s_x = [SS[f"s_x{i}"] for i in range(8)]
        s_yp = [SS["s_yp0"], SS["s_yp1"]]
        s_ab = [SS["s_ab0"], SS["s_ab1"]]
        s_aw = [SS["s_aw0"], SS["s_aw1"], SS["s_aw2"], SS["s_aw3"]]
        s_yw = [SS["s_yw0"], SS["s_yw1"]]
        s_os = [SS["s_os0"], SS["s_os1"]]
        s_wt = [SS["s_wt0"], SS["s_wt1"]]
        s_msk, s_qg, s_xw = SS["s_msk"], SS["s_qg"], SS["s_xw"]
        s_xwp = [SS["s_xw"], SS["s_xw1"]]
        s_yb = [SS["s_yb0"], SS["s_yb1"], SS["s_yb2"]]
        s_gW, s_vbf = SS["s_gW"], SS["s_vbf"]
        pe, dve, act, pool = SS["s_pe"], SS["s_dve"], SS["s_act"], SS["s_pool"]

        block = OS.enter_context(nc.Block())

        with contextlib.ExitStack() as P0:
            iota0 = sb(P0, "iota0", [128, 128], F32)
            ioh = sb(P0, "ioh", [4, HC], F32)
            oh_a = sb(P0, "oh_a", [4, HC], F32)
            oh_b = sb(P0, "oh_b", [4, HC], F32)
            onehot = sb(P0, "onehot", [4, HC], BF16)
            ones4 = sb(P0, "ones4", [4, 128], F32)
            onescol = sb(P0, "onescol", [128, 1], BF16)
            bvqf = sb(P0, "bvqf", [128, HC], F32)
            bvkf = sb(P0, "bvkf", [128, HC], F32)
            bvvf = sb(P0, "bvvf", [128, HC], F32)
            absb = sb(P0, "absb", [128, 12], BF16)
            absr = sb(P0, "absr", [128, 12], F32)
            alphas = sb(P0, "alphas", [4, 3], F32)
            alrep = sb(P0, "alrep", [4, 128 * 3], BF16)
            abq = sb(P0, "abq", [128, HC], F32)
            abk = sb(P0, "abk", [128, HC], F32)
            abv = sb(P0, "abv", [128, HC], F32)
            sgn = sb(P0, "sgn", [128, HC], F32)
            zro = sb(P0, "zro", [128, HC], F32)
            bndq = sb(P0, "bndq", [128, HC], F32)
            bndk = sb(P0, "bndk", [128, HC], F32)
            bndv = sb(P0, "bndv", [128, HC], BF16)
            vbtmp = sb(P0, "vbtmp", [HC, 128], BF16)
            p0a = ps(P0, "p0a", [128, 512])
            p0b = ps(P0, "p0b", [128, 512])

            def vstep(v, ins):
                dve.inc(ins, 1)
                v.wait_ge(dve.h, dve.n)

            @block.gpsimd
            def _(gp):
                gp.iota(iota0[:], [[1, 128]], base=0, channel_multiplier=-1,
                        allow_small_or_imprecise_dtypes=True)
                gp.iota(ioh[:], [[1, HC]], base=0, channel_multiplier=-CPH,
                        allow_small_or_imprecise_dtypes=True)
                gp.memset(epsv[:], 1e-5)
                gp.memset(ones4[:], 1.0)
                gp.memset(onescol[:], 1.0)
                pool.inc(gp.memset(onesrow[:], 1.0), 1)

            @block.sync
            def _(sy):
                s_msk.inc(sy.dma_start(out=bvqf[:], in_=bvq_in), 16)
                s_msk.inc(sy.dma_start(out=bvkf[:], in_=bvk_in), 16)
                s_msk.inc(sy.dma_start(out=bvvf[:], in_=bvv_in), 16)

            @block.vector
            def _(v):
                pool.wait(v)
                s_msk.wait(v)
                vstep(v, v.tensor_scalar(ident[:], iota0[:], 0.0, None, op0=ALU.is_equal))
                vstep(v, v.tensor_scalar(oh_a[:], ioh[:], -0.5, None, op0=ALU.is_ge))
                vstep(v, v.tensor_scalar(oh_b[:], ioh[:], CPH - 0.5, None, op0=ALU.is_le))
                vstep(v, v.tensor_tensor(onehot[:], oh_a[:], oh_b[:], op=ALU.mult))
                for t, bvf in enumerate([bvqf, bvkf, bvvf]):
                    for h in range(4):
                        vstep(v, v.tensor_reduce(
                            absr[:, t * 4 + h:t * 4 + h + 1],
                            bvf[:, h * CPH:(h + 1) * CPH], axis=AXX, op=ALU.add,
                            apply_absolute_value=True))
                vstep(v, v.tensor_copy(absb[:], absr[:]))

            @block.tensor
            def _(t):
                t.wait_ge(dve.h, dve.n)
                for tt in range(3):
                    ins = t.matmul(p0a[0:4, tt:tt + 1], absb[:, tt * 4:(tt + 1) * 4],
                                   onescol[:], start=True, stop=True)
                pe.inc(ins, 1)

            @block.vector
            def _(v):
                v.wait_ge(pe.h, pe.n)
                vstep(v, v.tensor_scalar(alphas[:], p0a[0:4, 0:3], 1.0 / HD, None,
                                         op0=ALU.mult))
                for t in range(3):
                    vstep(v, v.tensor_scalar(alrep[:, t * 128:(t + 1) * 128], ones4[:],
                                             alphas[:, t:t + 1], None, op0=ALU.mult))

            @block.tensor
            def _(t):
                t.wait_ge(dve.h, dve.n)
                pe.inc(t.matmul(p0a[:, 0:HC], alrep[:, 0:128], onehot[:],
                                start=True, stop=True), 1)
                pe.inc(t.matmul(p0b[:, 0:HC], alrep[:, 128:256], onehot[:],
                                start=True, stop=True), 1)

            @block.vector
            def _(v):
                v.wait_ge(pe.h, pe.n)
                vstep(v, v.tensor_copy(abq[:], p0a[:, 0:HC]))
                vstep(v, v.tensor_copy(abk[:], p0b[:, 0:HC]))

            @block.tensor
            def _(t):
                t.wait_ge(dve.h, dve.n)
                pe.inc(t.matmul(p0a[:, 0:HC], alrep[:, 256:384], onehot[:],
                                start=True, stop=True), 1)

            @block.vector
            def _(v):
                v.wait_ge(pe.h, pe.n)
                vstep(v, v.tensor_copy(abv[:], p0a[:, 0:HC]))

            for bvf, ab, out_bind, last in [(bvqf, abq, bndq, False),
                                            (bvkf, abk, bndk, True),
                                            (bvvf, abv, bndv, None)]:
                @block.scalar
                def _(s, bvf=bvf):
                    s.wait_ge(dve.h, dve.n)
                    act.inc(s.activation(sgn[:], bvf[:], AF.Sign), 1)

                @block.vector
                def _(v, bvf=bvf, ab=ab, out_bind=out_bind, last=last):
                    v.wait_ge(act.h, act.n)
                    vstep(v, v.tensor_scalar(zro[:], bvf[:], 0.0, None, op0=ALU.is_equal))
                    vstep(v, v.tensor_tensor(sgn[:], sgn[:], zro[:], op=ALU.add))
                    vstep(v, v.tensor_tensor(out_bind[:], sgn[:], ab[:], op=ALU.mult))
                    if last is True:
                        vstep(v, v.scalar_tensor_tensor(g_all[:], bndq[:], 0.125,
                                                        bndk[:], op0=ALU.mult,
                                                        op1=ALU.mult))

            @block.tensor
            def _(t):
                t.wait_ge(dve.h, dve.n)
                pe.inc(t.matmul(p0b[0:HC, 0:128], bndv[:], ident[:],
                                start=True, stop=True), 1)

            @block.scalar
            def _(s):
                s.wait_ge(pe.h, pe.n)
                act.inc(s.activation(vbtmp[:], p0b[0:HC, 0:128], AF.Copy), 1)

            @block.gpsimd
            def _(gp):
                gp.wait_ge(act.h, act.n)
                s_gW.inc(gp.dma_start(
                    out=vb_sc.rearrange("o (c p) -> (o c) p", p=128), in_=vbtmp[:]), 16)

        with contextlib.ExitStack() as P1:
            xbuf = [sb(P1, f"xbuf{i}", [128, D], F32) for i in range(4)]
            xhb = [sb(P1, f"xhb{i}", [128, D], BF16) for i in range(2)]
            st6 = [sb(P1, f"st6_{i}", [128, 6 * (D // 512)], F32) for i in range(2)]
            mv4 = [sb(P1, f"mv4_{i}", [128, 8], F32) for i in range(2)]
            nmn4 = [sb(P1, f"nmn4_{i}", [128, 4], F32) for i in range(2)]
            lnv4 = [sb(P1, f"lnv4_{i}", [128, 4], F32) for i in range(2)]
            rsd4 = [sb(P1, f"rsd4_{i}", [128, 4], F32) for i in range(2)]
            nmr4 = [sb(P1, f"nmr4_{i}", [128, 4], F32) for i in range(2)]
            NCH = D // 512
            assert NSEQ % 4 == 0
            NGR = NSEQ // 4

            d1 = dve.n
            a1 = act.n
            x0 = [s.n for s in s_x]
            w0 = [s_xwp[0].n, s_xwp[1].n]
            msk0 = s_msk.n

            def BNE(i):
                return d1 + 10 * (i // 4) + 2 * (i % 4) + 1

            def NORME(i):
                return a1 + 6 * (i // 4) + 3 + (i % 4)

            def xload(eng, i):
                if i >= 4:
                    eng.wait_ge(dve.h, BNE(i - 4))
                    eng.wait_ge(act.h, NORME(i - 4))
                bix = i % 4
                s_x[bix].inc(eng.dma_start(out=xbuf[bix][:],
                                           in_=xs[i * 128:(i + 1) * 128, :]), 16)

            @block.sync
            def _(sy):
                pe.wait(sy)
                act.wait(sy)
                dve.wait(sy)
                s_gW.wait(sy)
                for i in range(NSEQ):
                    xload(sy, i)

            @block.gpsimd
            def _(gp):
                pe.wait(gp)
                act.wait(gp)
                dve.wait(gp)
                s_gW.wait(gp)
                gp.wait_ge(s_msk.h, msk0)
                for qg in range(NJ // 4):
                    t_need = M + 4 * qg + 3
                    gp.wait_ge(s_xwp[0].h, w0[0] + 16 * (t_need // 2 + 1))
                    gp.wait_ge(s_xwp[1].h, w0[1] + 16 * ((t_need + 1) // 2))
                    oview = xq_sc[4 * qg * 128:(4 * qg + 4) * 128, :].rearrange(
                        "(j r) d -> j r d", r=128)
                    ivA = xh_sc[4 * qg * 128:(4 * qg + 4) * 128, :].rearrange(
                        "(j r) d -> j r d", r=128)
                    ivB = xh_sc[(M + 4 * qg) * 128:(M + 4 * qg + 4) * 128, :].rearrange(
                        "(j r) d -> j r d", r=128)
                    s_ab[qg % 2].inc(gp.dma_start(
                        out=oview[:, 0:64, :], in_=ivA[:, 0:64, :]), 16)
                    s_ab[qg % 2].inc(gp.dma_start(
                        out=oview[:, 64:128, :], in_=ivB[:, 0:64, :]), 16)

            @block.vector
            def _(v):
                for g in range(NGR):
                    for t in range(4):
                        i = 4 * g + t
                        bix = i % 4
                        v.wait_ge(s_x[bix].h, x0[bix] + 16 * (i // 4 + 1))
                        if g >= 1 and t == 0:
                            v.wait_ge(act.h, a1 + 6 * (g - 1) + 2)
                        for ch in range(NCH):
                            ins = v.bn_stats(st6[i % 2][:, ch * 6:(ch + 1) * 6],
                                             xbuf[bix][:, ch * 512:(ch + 1) * 512])
                        dve.inc(ins, 1)
                        v.wait_ge(dve.h, d1 + 10 * g + 2 * t + 1)
                        dve.inc(v.bn_aggr(mv4[g % 2][:, 2 * t:2 * t + 2],
                                          st6[i % 2][:]), 1)
                    v.wait_ge(dve.h, dve.n)
                    mvv = mv4[g % 2][:].rearrange("p (t two) -> p t two", two=2)
                    dve.inc(v.tensor_scalar(nmn4[g % 2][:], mvv[:, :, 0], -1.0,
                                            None, op0=ALU.mult), 1)
                    v.wait_ge(dve.h, dve.n)
                    v.wait_ge(act.h, a1 + 6 * g + 2)
                    dve.inc(v.tensor_tensor(nmr4[g % 2][:], nmn4[g % 2][:],
                                            rsd4[g % 2][:], op=ALU.mult), 1)

            @block.scalar
            def _(s):
                pe.wait(s)
                s_gW.wait(s)
                for g in range(NGR):
                    mvv = mv4[g % 2][:].rearrange("p (t two) -> p t two", two=2)
                    s.wait_ge(dve.h, d1 + 10 * g + 8)
                    act.inc(s.activation(lnv4[g % 2][:], mvv[:, :, 1], AF.Ln,
                                         bias=epsv[:], scale=1.0), 1)
                    s.wait_ge(act.h, act.n)
                    act.inc(s.activation(rsd4[g % 2][:], lnv4[g % 2][:], AF.Exp,
                                         scale=-0.5), 1)
                    s.wait_ge(dve.h, d1 + 10 * g + 10)
                    for t in range(4):
                        i = 4 * g + t
                        p = i % 2
                        s.wait_ge(s_xwp[p].h, w0[p] + 16 * (i // 2))
                        act.inc(s.activation(xhb[p][:], xbuf[i % 4][:], AF.Identity,
                                             bias=nmr4[g % 2][:, t:t + 1],
                                             scale=rsd4[g % 2][:, t:t + 1]), 1)
                        s.wait_ge(act.h, act.n)
                        s_xwp[p].inc(s.dma_start(
                            out=xh_sc[i * 128:(i + 1) * 128, :], in_=xhb[p][:]), 16)

            assert dve.n == d1 + 10 * NGR
            assert act.n == a1 + 6 * NGR

        with contextlib.ExitStack() as P2:
            maskt = sb(P2, "maskt", [128, NMASK * PSA_W], BF16)
            vbflat = sb(P2, "vbflat", [1, D], BF16)
            vb_full = sb(P2, "vb_full", [128, D], BF16)
            Kh = sb(P2, "Kh", [128, CPH * T], BF16)
            Vh = sb(P2, "Vh", [128, NSEQ * HD], BF16)
            attnT = sb(P2, "attnT", [128, NKmax * PSA_W], BF16)
            qgb = sb(P2, "qgb", [128, CPH * PSA_W], BF16)
            astg = [sb(P2, f"astg{i}", [128, HD], BF16) for i in range(4)]
            pOut = [ps(P2, f"pOut{i}", [128, max(1024, HD)]) for i in range(4)]

            pe2 = pe.n
            act2 = act.n
            dve2 = dve.n
            peVB = pe.n
            actVB = act.n
            qg0 = s_qg.n
            xw_all = [s_xwp[0].n, s_xwp[1].n]
            kt00 = s_kt0.n
            ktR0 = s_ktR.n
            aw0 = [s.n for s in s_aw]

            NVB = D // 512

            def peU(u):
                return pe2 + NVB + sum(NK(uu % NGRP) + 1 for uu in range(u))

            def actU(u):
                return act2 + NVB + sum(NK(uu % NGRP) + 4 for uu in range(u))

            def PASSA(u, k):
                return peU(u) + k + 1

            def PASSB(u):
                return peU(u) + NK(u % NGRP) + 1

            def SIG(u, k):
                return actU(u) + k + 1

            def EVAC(u, tt):
                return actU(u) + NK(u % NGRP) + tt + 1

            @block.sync
            def _(sy):
                sy.wait_ge(act.h, act2)
                sy.wait_ge(dve.h, dve2)
                sy.wait_ge(pe.h, pe2)
                s_gW.wait(sy)
                sy.wait_ge(s_xwp[0].h, xw_all[0])
                sy.wait_ge(s_xwp[1].h, xw_all[1])
                sy.wait_ge(s_ab[0].h, s_ab[0].n)
                sy.wait_ge(s_ab[1].h, s_ab[1].n)
                s_vbf.inc(sy.dma_start(out=vbflat[:], in_=vb_sc), 16)
                s_msk.inc(sy.dma_start(
                    out=maskt[:].rearrange("p (m t) -> p m t", m=NMASK),
                    in_=masks_in.rearrange("m p t -> p m t")), 16)
                for u in range(NU):
                    h, g = u // NGRP, u % NGRP
                    if u > 0:
                        sy.wait_ge(pe.h, PASSA(u - 1, NK((u - 1) % NGRP) - 1))
                    if g == 0 and h == 0:
                        sy.wait_ge(s_xwp[0].h, xw_all[0])
                        sy.wait_ge(s_xwp[1].h, xw_all[1])
                    for c in range(CPH):
                        s_qg.inc(sy.dma_start(
                            out=qgb[:, c * PSA_W:(c + 1) * PSA_W],
                            in_=xq_sc[g * PSA_W:(g + 1) * PSA_W,
                                      (h * CPH + c) * 128:(h * CPH + c + 1) * 128],
                            transpose=True), 16)
                    if g == 0:
                        for c in range(CPH):
                            s_kt0.inc(sy.dma_start(
                                out=Kh[:, c * T: c * T + 512],
                                in_=xh_sc[0:512,
                                          (h * CPH + c) * 128:(h * CPH + c + 1) * 128],
                                transpose=True), 16)
                        for c in range(CPH):
                            s_ktR.inc(sy.dma_start(
                                out=Kh[:, c * T + 512: (c + 1) * T],
                                in_=xh_sc[512:T,
                                          (h * CPH + c) * 128:(h * CPH + c + 1) * 128],
                                transpose=True), 16)

            qgdone = {}
            vbdone = {}
            VSTAGES = []
            for si, (kb0, kb1) in enumerate([(0, 4), (4, min(16, NSEQ)), (16, NSEQ)]):
                if kb1 > kb0:
                    VSTAGES.append((kb0, kb1, [s_vt0, s_vtM, s_vtR][si]))
            vst0 = {id(sem): sem.n for (_, _, sem) in VSTAGES}

            @block.vector
            def _(v):
                v.wait_ge(act.h, actVB + NVB)
                for u in range(NU):
                    h, g = u // NGRP, u % NGRP
                    v.wait_ge(s_qg.h, qg0 + 16 * CPH * (u + 1))
                    for c in range(CPH):
                        ins = v.tensor_scalar(qgb[:, c * PSA_W:(c + 1) * PSA_W],
                                              qgb[:, c * PSA_W:(c + 1) * PSA_W],
                                              g_all[:, h * CPH + c:h * CPH + c + 1],
                                              None, op0=ALU.mult)
                    dve.inc(ins, 1)
                    qgdone[u] = dve.n
                    if g == 0:
                        for si, (kb0, kb1, sem) in enumerate(VSTAGES):
                            v.wait_ge(sem.h, vst0[id(sem)] + 16 * (h + 1))
                            for kt in range(kb0, kb1):
                                off = kt * HD
                                ins = v.tensor_tensor(
                                    Vh[:, off:off + HD], Vh[:, off:off + HD],
                                    vb_full[:, h * HD:(h + 1) * HD], op=ALU.mult)
                            dve.inc(ins, 1)
                            vbdone[(h, kb0)] = dve.n

            @block.tensor
            def _(t):
                s_msk.wait(t)
                s_vbf.wait(t)
                for o in range(NVB):
                    if o >= 2:
                        t.wait_ge(act.h, actVB + (o - 2) + 1)
                    pe.inc(t.matmul(pOut[o % 2][:, 0:512], onesrow[:],
                                    vbflat[:, o * 512:(o + 1) * 512],
                                    start=True, stop=True), 1)
                for u in range(NU):
                    h, g = u // NGRP, u % NGRP
                    nk = NK(g)
                    t.wait_ge(dve.h, qgdone[u])
                    for k in range(nk):
                        midx = k - GS * g
                        compact = GS <= midx < M
                        if k == 0:
                            t.wait_ge(s_kt0.h, kt00 + 16 * CPH * (h + 1))
                        elif k == 4:
                            t.wait_ge(s_ktR.h, ktR0 + 16 * CPH * (h + 1))
                        if k < 4:
                            if u > 0:
                                t.wait_ge(act.h, EVAC(u - 1, 0 if k < 2 else 1))
                            else:
                                t.wait_ge(act.h, actVB + NVB)
                        else:
                            t.wait_ge(act.h, SIG(u, k - 4))
                        has_mask = 0 <= midx < GS or M <= midx < M + GS
                        mdix = midx if midx < GS else GS + (midx - M)
                        if compact:
                            psA = pOut[(k % 4) // 2][:, (k % 2) * PSA_W:(k % 2) * PSA_W + 256]
                            for c in range(CPH):
                                qgu = qgb[:, c * PSA_W:(c + 1) * PSA_W].rearrange(
                                    "p (j q) -> p j q", q=128)[:, :, 64:128]
                                ins = t.matmul(psA,
                                               Kh[:, c * T + k * 128: c * T + (k + 1) * 128],
                                               qgu, start=(c == 0), stop=(c == CPH - 1))
                        else:
                            psA = pOut[(k % 4) // 2][:, (k % 2) * PSA_W:(k % 2 + 1) * PSA_W]
                            for c in range(CPH):
                                ins = t.matmul(psA,
                                               Kh[:, c * T + k * 128: c * T + (k + 1) * 128],
                                               qgb[:, c * PSA_W:(c + 1) * PSA_W],
                                               start=(c == 0),
                                               stop=(c == CPH - 1 and not has_mask))
                            if has_mask:
                                ins = t.matmul(psA, ident[:],
                                               maskt[:, mdix * PSA_W:(mdix + 1) * PSA_W],
                                               start=False, stop=True)
                        pe.inc(ins, 1)
                    t.wait_ge(act.h, SIG(u, nk - 1))
                    for k in range(nk):
                        midx = k - GS * g
                        compact = GS <= midx < M
                        if (h, k) in vbdone:
                            t.wait_ge(dve.h, vbdone[(h, k)])
                        for tt in range(GS):
                            for dh in range(NDH):
                                if compact:
                                    ins = t.matmul(
                                        pOut[tt][64:128, dh * DHW:(dh + 1) * DHW],
                                        attnT[:, k * PSA_W + tt * 64: k * PSA_W + (tt + 1) * 64],
                                        Vh[:, k * HD + dh * DHW: k * HD + (dh + 1) * DHW],
                                        start=False, stop=(k == nk - 1))
                                else:
                                    ins = t.matmul(
                                        pOut[tt][:, dh * DHW:(dh + 1) * DHW],
                                        attnT[:, k * PSA_W + tt * 128: k * PSA_W + (tt + 1) * 128],
                                        Vh[:, k * HD + dh * DHW: k * HD + (dh + 1) * DHW],
                                        start=(k == 0), stop=(k == nk - 1))
                    pe.inc(ins, 1)

            aw_after = {}

            @block.scalar
            def _(s):
                for o in range(NVB):
                    s.wait_ge(pe.h, peVB + o + 1)
                    act.inc(s.activation(vb_full[:, o * 512:(o + 1) * 512],
                                         pOut[o % 2][:, 0:512], AF.Copy), 1)
                for u in range(NU):
                    h, g = u // NGRP, u % NGRP
                    nk = NK(g)
                    if g == 0:
                        if h > 0:
                            s.wait_ge(pe.h, peU(u))
                        else:
                            s.wait_ge(s_xwp[0].h, xw_all[0])
                            s.wait_ge(s_xwp[1].h, xw_all[1])
                        for (kb0, kb1, sem) in VSTAGES:
                            sem.inc(s.dma_start(
                                out=Vh[:, kb0 * HD:kb1 * HD].rearrange(
                                    "p (k d) -> p k d", k=kb1 - kb0),
                                in_=xh_sc[kb0 * 128:kb1 * 128,
                                          h * HD:(h + 1) * HD].rearrange(
                                    "(k p) d -> p k d", p=128)), 16)
                    for k in range(nk):
                        midx = k - GS * g
                        w = 256 if GS <= midx < M else PSA_W
                        s.wait_ge(pe.h, PASSA(u, k))
                        act.inc(s.activation(
                            attnT[:, k * PSA_W:k * PSA_W + w],
                            pOut[(k % 4) // 2][:, (k % 2) * PSA_W:(k % 2) * PSA_W + w],
                            AF.Sigmoid), 1)
                    s.wait_ge(pe.h, PASSB(u))
                    if u > 0:
                        gp_prev = (u - 1) % NGRP
                        s.wait_ge(s_aw[gp_prev].h, aw_after[u - 1])
                    for tt in range(GS):
                        act.inc(s.activation(astg[tt][:], pOut[tt][:, 0:HD], AF.Copy), 1)
                        s.wait_ge(act.h, act.n)
                        row = g * PSA_W + tt * 128
                        s_aw[g].inc(s.dma_start(
                            out=a_sc[row:row + 128, h * HD:(h + 1) * HD],
                            in_=astg[tt][:]), 16)
                    aw_after[u] = s_aw[g].n

            assert pe.n == peU(NU), (pe.n, peU(NU))
            assert act.n == actU(NU), (act.n, actU(NU))

        with contextlib.ExitStack() as P3S:
            yfm = sb(P3S, "yfm", [128, HC * TH], BF16)
            hgTa = sb(P3S, "hgTa", [112, TH], BF16)
            hgTb = sb(P3S, "hgTb", [112, TH], BF16)

            with contextlib.ExitStack() as PP:
                ybuf = [sb(PP, f"ybuf{i}", [128, D], F32) for i in range(2)]
                yhb = [sb(PP, f"yhb{i}", [128, D], BF16) for i in range(2)]
                abuf = [sb(PP, f"abuf{i}", [128, D], BF16) for i in range(2)]
                st6p = [sb(PP, f"st6p{i}", [128, 6 * (D // 512)], F32) for i in range(2)]
                mvp = [sb(PP, f"mvp{i}", [128, 2], F32) for i in range(2)]
                nmnp = [sb(PP, f"nmnp{i}", [128, 1], F32) for i in range(2)]
                sqvp = [sb(PP, f"sqvp{i}", [128, 1], F32) for i in range(2)]
                rsdp = [sb(PP, f"rsdp{i}", [128, 1], F32) for i in range(2)]
                nmrp = [sb(PP, f"nmrp{i}", [128, 1], F32) for i in range(2)]
                pq = [ps(PP, f"pq{i}", [128, 512]) for i in range(8)]
                NCH = D // 512
                DSP = max(512, min(D - 512, (D * 5 // 8) // 512 * 512))

                pe3 = pe.n
                act3 = act.n
                dve3 = dve.n
                pool3 = pool.n
                yw0v = [s_yw[0].n, s_yw[1].n]
                awF = [s.n for s in s_aw]
                yb0v = [s.n for s in s_yb]
                ac0v = [s.n for s in s_ac]

                DVE_PJ, ACT_PJ, PE_PJ = 3, 13, 2

                def ADDS(j):
                    return dve3 + DVE_PJ * j + 1

                def AGGRP(j):
                    return dve3 + DVE_PJ * j + 3

                def NORMP(j):
                    return act3 + ACT_PJ * j + 5

                def EVA(j, r, q):
                    return act3 + ACT_PJ * j + 5 + r * 4 + q + 1

                def TRND(j, r):
                    return pe3 + PE_PJ * j + r + 1

                def ywcnt(j):
                    return yw0v[j % 2] + 16 * (j // 2 + 1)

                @block.sync
                def _(sy):
                    sy.wait_ge(pe.h, pe3)
                    sy.wait_ge(act.h, act3)
                    sy.wait_ge(dve.h, dve3)
                    for g4 in range(4):
                        sy.wait_ge(s_aw[g4].h, awF[g4])
                    for j in range(NJ):
                        p = j % 2
                        if j >= 2:
                            sy.wait_ge(dve.h, AGGRP(j - 2))
                            sy.wait_ge(act.h, NORMP(j - 2))
                            sy.wait_ge(s_yw[p].h, ywcnt(j - 2))
                        s_yb[p].inc(sy.dma_start(
                            out=ybuf[p][:], in_=xq[j * 128:(j + 1) * 128, :]), 16)

                @block.gpsimd
                def _(gp):
                    gp.wait_ge(pe.h, pe3)
                    gp.wait_ge(act.h, act3)
                    for g4 in range(4):
                        gp.wait_ge(s_aw[g4].h, awF[g4])
                    def a_load(jj):
                        pj = jj % 2
                        if jj >= 2:
                            gp.wait_ge(dve.h, ADDS(jj - 2))
                            gp.wait_ge(pool.h, pool3 + jj - 1)
                        s_ac[pj].inc(gp.dma_start(
                            out=abuf[pj][:], in_=a_sc[jj * 128:(jj + 1) * 128, :]), 16)
                    a_load(0)
                    for j in range(NJ):
                        p = j % 2
                        if j + 1 < NJ:
                            a_load(j + 1)
                        gp.wait_ge(s_yb[p].h, yb0v[p] + 16 * (j // 2 + 1))
                        gp.wait_ge(s_ac[p].h, ac0v[p] + 16 * (j // 2 + 1))
                        pool.inc(gp.tensor_tensor(ybuf[p][:, DSP:D],
                                                  ybuf[p][:, DSP:D],
                                                  abuf[p][:, DSP:D], op=ALU.add), 1)

                @block.vector
                def _(v):
                    for j in range(NJ):
                        p = j % 2
                        v.wait_ge(s_yb[p].h, yb0v[p] + 16 * (j // 2 + 1))
                        v.wait_ge(s_ac[p].h, ac0v[p] + 16 * (j // 2 + 1))
                        dve.inc(v.tensor_tensor(ybuf[p][:, 0:DSP], ybuf[p][:, 0:DSP],
                                                abuf[p][:, 0:DSP], op=ALU.add), 1)
                        v.wait_ge(dve.h, ADDS(j))
                        v.wait_ge(pool.h, pool3 + j + 1)
                        for ch in range(NCH):
                            ins = v.bn_stats(st6p[j % 2][:, ch * 6:(ch + 1) * 6],
                                             ybuf[p][:, ch * 512:(ch + 1) * 512])
                        dve.inc(ins, 1)
                        v.wait_ge(dve.h, dve.n)
                        if j >= 2:
                            v.wait_ge(act.h, act3 + ACT_PJ * (j - 2) + 2)
                        dve.inc(v.bn_aggr(mvp[j % 2][:], st6p[j % 2][:]), 1)

                @block.scalar
                def _(s):
                    for j in range(NJ):
                        p = j % 2
                        s.wait_ge(dve.h, AGGRP(j))
                        act.inc(s.activation(nmnp[p][:], mvp[p][:, 0:1], AF.Identity,
                                             scale=-1.0), 1)
                        act.inc(s.activation(sqvp[p][:], mvp[p][:, 1:2], AF.Ln,
                                             bias=epsv[:], scale=1.0), 1)
                        s.wait_ge(act.h, act.n)
                        act.inc(s.activation(rsdp[p][:], sqvp[p][:], AF.Exp,
                                             scale=-0.5), 1)
                        s.wait_ge(act.h, act.n)
                        act.inc(s.activation(nmrp[p][:], nmnp[p][:], AF.Identity,
                                             scale=rsdp[p][:]), 1)
                        s.wait_ge(act.h, act.n)
                        if j >= 2:
                            s.wait_ge(pe.h, TRND(j - 2, 1))
                        act.inc(s.activation(yhb[j % 2][:], ybuf[p][:], AF.Identity,
                                             bias=nmrp[p][:], scale=rsdp[p][:]), 1)
                        s.wait_ge(act.h, act.n)
                        s.wait_ge(s_yw[p].h, yw0v[p] + 16 * (j // 2))
                        s_yw[p].inc(s.dma_start(out=out_hbm[j * 128:(j + 1) * 128, :],
                                                in_=ybuf[p][:]), 16)
                        for r in range(2):
                            s.wait_ge(pe.h, TRND(j, r))
                            for q in range(4):
                                cbase = r * (HC // 2) + q * CPB
                                act.inc(s.activation(
                                    yfm[:].rearrange("p (c t) -> p c t", t=TH)[
                                        :, cbase:cbase + CPB, j * 128:(j + 1) * 128],
                                    pq[r * 4 + q][:, 0:CPB * 128].rearrange(
                                        "p (c t) -> p c t", t=128),
                                    AF.Copy), 1)

                @block.tensor
                def _(t):
                    for j in range(NJ):
                        t.wait_ge(act.h, NORMP(j))
                        for r in range(2):
                            if j >= 1:
                                t.wait_ge(act.h, EVA(j - 1, r, 3))
                            for ci in range(HC // 2):
                                c = r * (HC // 2) + ci
                                q, col = ci // CPB, ci % CPB
                                ins = t.matmul(
                                    pq[r * 4 + q][:, col * 128:(col + 1) * 128],
                                    yhb[j % 2][:, c * 128:(c + 1) * 128],
                                    ident[:], start=True, stop=True)
                            pe.inc(ins, 1)

                assert dve.n == dve3 + DVE_PJ * NJ
                assert act.n == act3 + ACT_PJ * NJ
                assert pe.n == pe3 + PE_PJ * NJ

            with contextlib.ExitStack() as PG:
                RANK = cfg["RANK"]
                agb = sb(PG, "agb", [128, HC * NARE], BF16)
                lgt_all = sb(PG, "lgt_all", [128, NJ * NARE], F32)
                m1a = sb(PG, "m1a", [128, NJ], F32)
                m2a = sb(PG, "m2a", [128, NJ], F32)
                d12a = sb(PG, "d12a", [128, NJ], F32)
                p1a = sb(PG, "p1a", [128, NJ], F32)
                p2a = sb(PG, "p2a", [128, NJ], F32)
                km1a = sb(PG, "km1a", [128, NJ * NE], F32)
                km2a = sb(PG, "km2a", [128, NJ * NE], F32)
                l2a = sb(PG, "l2a", [128, NJ * NE], F32)
                g28a = sb(PG, "g28a", [128, NJ * NE], F32)
                hgball = sb(PG, "hgball", [128, NJ * NER], BF16)
                psL = [ps(PG, "psL0", [128, 512]), ps(PG, "psL1", [128, 512])]
                psT = [ps(PG, "psT0", [128, 512]), ps(PG, "psT1", [128, 512])]

                peG = pe.n
                actG = act.n
                dveG = dve.n
                gw0 = s_gW.n

                def LMM(j):
                    return peG + j + 1

                def TMM(j):
                    return peG + NJ + j + 1

                def LG(j):
                    return dveG + j + 1

                D_L = dveG + NJ
                D_M1 = D_L + 1
                D_KM1 = D_M1 + NJ
                D_L2 = D_KM1 + NJ
                D_M2 = D_L2 + 1
                D_KM2 = D_M2 + NJ
                D_D12 = D_KM2 + 1
                D_G28a = D_D12 + NJ
                D_G28b = D_G28a + NJ
                D_SC = D_G28b + 1
                D_HG = D_SC + RANK
                A_P = actG + 2

                def A_EV(j):
                    return A_P + 2 * (j + 1)

                @block.gpsimd
                def _(gp):
                    gp.wait_ge(act.h, actG)
                    gp.wait_ge(dve.h, dveG)
                    gp.wait_ge(pe.h, peG)
                    gp.wait_ge(s_yw[0].h, s_yw[0].n)
                    gp.wait_ge(s_yw[1].h, s_yw[1].n)
                    for sy3 in s_yb + s_ac:
                        gp.wait_ge(sy3.h, sy3.n)
                    s_gW.inc(gp.dma_start(
                        out=agb[:].rearrange("p (c e) -> p c e", e=NARE),
                        in_=ag_in.rearrange("(c p) e -> p c e", p=128)), 16)

                @block.tensor
                def _(t):
                    t.wait_ge(s_gW.h, gw0 + 16)
                    t.wait_ge(act.h, actG)
                    for j in range(NJ):
                        if j >= 2:
                            t.wait_ge(dve.h, LG(j - 2))
                        for c in range(HC):
                            ins = t.matmul(psL[j % 2][:, 0:NARE],
                                           yfm[:, c * TH + j * 128: c * TH + (j + 1) * 128],
                                           agb[:, c * NARE:(c + 1) * NARE],
                                           start=(c == 0), stop=(c == HC - 1))
                        pe.inc(ins, 1)
                    for j in range(NJ):
                        t.wait_ge(dve.h, D_HG)
                        if j >= 2:
                            t.wait_ge(act.h, A_EV(j - 2))
                        t.matmul(psT[j % 2][0:112, 0:128],
                                 hgball[:, j * NER:j * NER + 112], ident[:],
                                 start=True, stop=True)
                        pe.inc(t.matmul(psT[j % 2][0:112, 128:256],
                                        hgball[:, j * NER + 112:(j + 1) * NER], ident[:],
                                        start=True, stop=True), 1)

                @block.vector
                def _(v):
                    lga = lgt_all[:].rearrange("p (j e) -> p j e", e=NARE)
                    for j in range(NJ):
                        v.wait_ge(pe.h, LMM(j))
                        dve.inc(v.tensor_copy(lgt_all[:, j * NARE:(j + 1) * NARE],
                                              psL[j % 2][:, 0:NARE]), 1)
                    v.wait_ge(dve.h, D_L)
                    dve.inc(v.reduce_max(m1a[:], lga[:, :, 0:NE], axis=AXX), 1)
                    v.wait_ge(dve.h, D_M1)
                    for j in range(NJ):
                        dve.inc(v.tensor_scalar(km1a[:, j * NE:(j + 1) * NE],
                                                lgt_all[:, j * NARE:j * NARE + NE],
                                                m1a[:, j:j + 1], None, op0=ALU.is_ge), 1)
                    v.wait_ge(dve.h, D_KM1)
                    for j in range(NJ):
                        dve.inc(v.scalar_tensor_tensor(
                            l2a[:, j * NE:(j + 1) * NE], km1a[:, j * NE:(j + 1) * NE],
                            -BIGL, lgt_all[:, j * NARE:j * NARE + NE],
                            op0=ALU.mult, op1=ALU.add), 1)
                    v.wait_ge(dve.h, D_L2)
                    dve.inc(v.reduce_max(m2a[:], l2a[:].rearrange(
                        "p (j e) -> p j e", e=NE), axis=AXX), 1)
                    v.wait_ge(dve.h, D_M2)
                    for j in range(NJ):
                        dve.inc(v.tensor_scalar(km2a[:, j * NE:(j + 1) * NE],
                                                l2a[:, j * NE:(j + 1) * NE],
                                                m2a[:, j:j + 1], None, op0=ALU.is_ge), 1)
                    v.wait_ge(dve.h, D_KM2)
                    dve.inc(v.tensor_tensor(d12a[:], m1a[:], m2a[:], op=ALU.subtract), 1)
                    v.wait_ge(act.h, A_P)
                    for j in range(NJ):
                        dve.inc(v.tensor_scalar(g28a[:, j * NE:(j + 1) * NE],
                                                km1a[:, j * NE:(j + 1) * NE],
                                                p1a[:, j:j + 1], None, op0=ALU.mult), 1)
                    v.wait_ge(dve.h, D_G28a)
                    for j in range(NJ):
                        dve.inc(v.scalar_tensor_tensor(
                            g28a[:, j * NE:(j + 1) * NE], km2a[:, j * NE:(j + 1) * NE],
                            p2a[:, j:j + 1], g28a[:, j * NE:(j + 1) * NE],
                            op0=ALU.mult, op1=ALU.add), 1)
                    v.wait_ge(dve.h, D_G28b)
                    dve.inc(v.tensor_scalar(g28a[:], g28a[:], 2.0, None, op0=ALU.mult), 1)
                    v.wait_ge(dve.h, D_SC)
                    hgv = hgball[:].rearrange("p (j e r) -> p j e r", e=NE, r=RANK)
                    g28v = g28a[:].rearrange("p (j e) -> p j e", e=NE)
                    lgh = lga[:, :, NE:NARE].rearrange("p j (e r) -> p j e r", r=RANK)
                    for r8 in range(RANK):
                        dve.inc(v.tensor_tensor(
                            hgv[:, :, :, r8], lgh[:, :, :, r8],
                            g28v[:, :, :], op=ALU.mult), 1)

                @block.scalar
                def _(s):
                    s.wait_ge(dve.h, D_D12)
                    act.inc(s.activation(p1a[:], d12a[:], AF.Sigmoid, scale=1.0), 1)
                    act.inc(s.activation(p2a[:], d12a[:], AF.Sigmoid, scale=-1.0), 1)
                    for j in range(NJ):
                        s.wait_ge(pe.h, TMM(j))
                        act.inc(s.activation(hgTa[:, j * 128:(j + 1) * 128],
                                             psT[j % 2][0:112, 0:128], AF.Copy), 1)
                        act.inc(s.activation(hgTb[:, j * 128:(j + 1) * 128],
                                             psT[j % 2][0:112, 128:256], AF.Copy), 1)

                assert dve.n == D_HG
                assert act.n == A_P + 2 * NJ
                assert pe.n == peG + 2 * NJ

            with contextlib.ExitStack() as PM:
                wtb = [sb(PM, f"wtb{i}", [128, HC * OGW], BF16) for i in range(2)]
                b2ab = sb(PM, "b2ab", [112, D], BF16)
                b2bb = sb(PM, "b2bb", [112, D], BF16)
                ypc = [sb(PM, f"ypc{i}", [128, OGW], F32) for i in range(2)]
                ost = [sb(PM, f"ost{i}", [128, OGW], F32) for i in range(2)]
                pm = [ps(PM, f"pm{i}", [128, 512]) for i in range(4)]

                peM = pe.n
                dveM = dve.n
                actM = act.n
                gwM = s_gW.n
                yp0v = [s_yp[0].n, s_yp[1].n]
                os0v = [s_os[0].n, s_os[1].n]
                wt0v = [s_wt[0].n, s_wt[1].n]
                NID = NOG * NJ

                def MME(i):
                    return peM + i + 1

                def EVE(i):
                    return dveM + i + 1

                @block.gpsimd
                def _(gp):
                    gp.wait_ge(act.h, actM)
                    gp.wait_ge(dve.h, dveM)
                    gp.wait_ge(pe.h, peM)
                    s_gW.inc(gp.dma_start(out=b2ab[:], in_=b2a_in), 16)
                    s_gW.inc(gp.dma_start(out=b2bb[:], in_=b2b_in), 16)
                    for og in range(NOG):
                        if og >= 2:
                            gp.wait_ge(pe.h, MME((og - 1) * NJ - 1))
                        for q in range(4):
                            s_wt[og % 2].inc(gp.dma_start(
                                out=wtb[og % 2][:, q * HCG * OGW:(q + 1) * HCG * OGW
                                                ].rearrange("p (c o) -> p c o", o=OGW),
                                in_=wt_in[q * HCG * 128:(q + 1) * HCG * 128,
                                          og * OGW:(og + 1) * OGW
                                          ].rearrange("(c p) o -> p c o", p=128)), 16)

                @block.tensor
                def _(t):
                    t.wait_ge(s_gW.h, gwM + 32)
                    t.wait_ge(act.h, actM)
                    t.wait_ge(dve.h, dveM)
                    for og in range(NOG):
                        t.wait_ge(s_wt[og % 2].h, wt0v[og % 2] + 64 * (og // 2 + 1))
                        for tt in range(NJ):
                            i = og * NJ + tt
                            if i >= 4:
                                t.wait_ge(dve.h, EVE(i - 4))
                            bank = pm[tt % 4]
                            for c in range(HC):
                                t.matmul(bank[:, 0:OGW],
                                         yfm[:, c * TH + tt * 128: c * TH + (tt + 1) * 128],
                                         wtb[og % 2][:, c * OGW:(c + 1) * OGW],
                                         start=(c == 0), stop=False)
                            t.matmul(bank[:, 0:OGW],
                                     hgTa[:, tt * 128:(tt + 1) * 128],
                                     b2ab[:, og * OGW:(og + 1) * OGW],
                                     start=False, stop=False)
                            pe.inc(t.matmul(bank[:, 0:OGW],
                                            hgTb[:, tt * 128:(tt + 1) * 128],
                                            b2bb[:, og * OGW:(og + 1) * OGW],
                                            start=False, stop=True), 1)

                @block.sync
                def _(sy):
                    sy.wait_ge(s_yw[0].h, s_yw[0].n)
                    sy.wait_ge(s_yw[1].h, s_yw[1].n)
                    sy.wait_ge(act.h, actM)
                    sy.wait_ge(dve.h, dveM)
                    sy.wait_ge(pe.h, peM)
                    for i in range(NID):
                        og, tt = i // NJ, i % NJ
                        p = i % 2
                        if i >= 2:
                            sy.wait_ge(dve.h, EVE(i - 2))
                        s_yp[p].inc(sy.dma_start(
                            out=ypc[p][:],
                            in_=out_hbm[tt * 128:(tt + 1) * 128,
                                        og * OGW:(og + 1) * OGW]), 16)

                @block.vector
                def _(v):
                    for i in range(NID):
                        og, tt = i // NJ, i % NJ
                        p = i % 2
                        v.wait_ge(pe.h, MME(i))
                        v.wait_ge(s_yp[p].h, yp0v[p] + 16 * (i // 2 + 1))
                        if i >= 2:
                            v.wait_ge(s_os[p].h, os0v[p] + 16 * (i // 2))
                        dve.inc(v.tensor_tensor(ost[p][:], pm[tt % 4][:, 0:OGW],
                                                ypc[p][:], op=ALU.add), 1)

                @block.scalar
                def _(s):
                    for i in range(NID):
                        og, tt = i // NJ, i % NJ
                        p = i % 2
                        s.wait_ge(dve.h, EVE(i))
                        s_os[p].inc(s.dma_start(
                            out=out_hbm[tt * 128:(tt + 1) * 128,
                                        og * OGW:(og + 1) * OGW],
                            in_=ost[p][:]), 16)

                @block.sync
                def _(sy):
                    sy.wait_ge(s_os[0].h, s_os[0].n)
                    sy.wait_ge(s_os[1].h, s_os[1].n)

    return nc


_PROG_CACHE = {}


def _get_program(key, cfg):
    if key not in _PROG_CACHE:
        _PROG_CACHE[key] = build_program(cfg)
    return _PROG_CACHE[key]


def run(cfg, x, bvq, bvk, bvv, W, gateW, loraA, loraB, core_ids=None, sim=False,
        sim_cores=2):
    in_maps, gmaps = stage_inputs(cfg, x, bvq, bvk, bvv, W, gateW, loraA, loraB)
    nc = _get_program((cfg["T"], cfg["D"]), cfg)
    if sim:
        from simrun import simulate
        results = simulate(nc, in_maps[:sim_cores], ["out"], num_cores=sim_cores)
    else:
        results = run_bass_kernel_spmd(
            nc, in_maps, core_ids=core_ids or list(range(8))).results
    B = x.shape[0]
    out = np.zeros((B, cfg["T"], cfg["D"]), dtype=np.float32)
    for c in range(len(results)):
        b, gmap = gmaps[c]
        out[b][gmap] = results[c]["out"]
    return out


def kernel(L0_attn_x, L0_attn_bv_q, L0_attn_bv_k, L0_attn_bv_v,
           L0_ffn_base_W, L0_ffn_gate_W, L0_ffn_lora_A, L0_ffn_lora_B):
    cfg = make_cfg(T=4096, D=4096)
    return run(cfg, np.asarray(L0_attn_x), np.asarray(L0_attn_bv_q),
               np.asarray(L0_attn_bv_k), np.asarray(L0_attn_bv_v),
               np.asarray(L0_ffn_base_W), np.asarray(L0_ffn_gate_W),
               np.asarray(L0_ffn_lora_A), np.asarray(L0_ffn_lora_B))


# version 19
# speedup vs baseline: 1.0303x; 1.0107x over previous
import contextlib
import numpy as np
import ml_dtypes
import concourse.bass as bass
import concourse.mybir as mybir
from concourse.bass_utils import run_bass_kernel_spmd

F32 = mybir.dt.float32
BF16 = mybir.dt.bfloat16
AF = mybir.ActivationFunctionType
ALU = mybir.AluOpType
AXX = mybir.AxisListType.X

NEG = -30000.0
BIGL = 1.0e9


def make_cfg(T=4096, D=4096):
    H = 4
    cfg = dict(T=T, D=D, H=H, HD=D // H)
    cfg["HC"] = D // 128
    cfg["CPH"] = cfg["HD"] // 128
    cfg["NSEQ"] = T // 128
    cfg["TH"] = T // 2
    cfg["NJ"] = cfg["TH"] // 128
    cfg["GS"] = 4
    assert cfg["NJ"] % cfg["GS"] == 0
    cfg["NGRP"] = cfg["NJ"] // cfg["GS"]
    cfg["M"] = cfg["TH"] // 128
    cfg["NMASK"] = 2 * cfg["GS"]
    cfg["NE"] = 28
    cfg["RANK"] = 8
    cfg["NER"] = 224
    cfg["NARE"] = 252
    cfg["OGW"] = 256
    cfg["NOG"] = D // cfg["OGW"]
    cfg["NDH"] = max(1, cfg["HD"] // 512)
    cfg["DHW"] = min(512, cfg["HD"])
    return cfg



def local_to_global_rows(cfg, parity):
    TH, NJ = cfg["TH"], cfg["NJ"]
    gmap = np.zeros(TH, dtype=np.int64)
    for j in range(NJ):
        for p in range(64):
            gmap[j * 128 + p] = 128 * j + 64 * parity + p
            gmap[j * 128 + 64 + p] = TH + 128 * j + 64 * parity + p
    return gmap


def build_masks(cfg, parity):
    GS, M = cfg["GS"], cfg["M"]
    out = np.zeros((cfg["NMASK"], 128, GS * 128), dtype=np.float32)
    s = np.arange(128)[:, None]
    srot = (s + 64 * parity) % 128
    p = np.arange(128)[None, :]
    lower = p < 64
    upper = ~lower
    for idx in range(cfg["NMASK"]):
        midx = idx if idx < GS else M + (idx - GS)
        for jj in range(GS):
            d = midx - jj
            if d < 0:
                continue
            blk = out[idx][:, jj * 128:(jj + 1) * 128]
            if d == 0:
                qrel = 64 * parity + p
                blk += np.where(lower & (srot > qrel), NEG, 0.0)
            elif d < M:
                blk += np.where(lower, NEG, 0.0)
            elif d == M:
                qrel = 64 * parity + (p - 64)
                blk += np.where(lower | (upper & (srot > qrel)), NEG, 0.0)
            else:
                blk += NEG
    return np.clip(out, NEG, 0.0).astype(ml_dtypes.bfloat16)


def bv_rlayout(bv):
    flat = np.ascontiguousarray(bv, dtype=np.float32).reshape(-1)
    return np.ascontiguousarray(flat.reshape(-1, 128).T)


def stage_inputs(cfg, x, bvq, bvk, bvv, W, gateW, loraA, loraB):
    D = cfg["D"]
    wt = np.ascontiguousarray(np.asarray(W).T, dtype=np.float32)
    A2 = np.ascontiguousarray(np.asarray(loraA).reshape(cfg["NE"] * cfg["RANK"], D))
    ag = np.ascontiguousarray(
        np.concatenate([np.asarray(gateW).T, A2.T], axis=1), dtype=np.float32)
    B2 = np.ascontiguousarray(
        np.transpose(np.asarray(loraB), (0, 2, 1)).reshape(cfg["NER"], D))
    bq, bk, bv_ = bv_rlayout(bvq), bv_rlayout(bvk), bv_rlayout(bvv)
    in_maps, gmaps = [], []
    for c in range(8):
        b, parity = c // 2, c & 1
        bb = min(b, x.shape[0] - 1)
        gmap = local_to_global_rows(cfg, parity)
        gmaps.append((bb, gmap))
        xr = np.asarray(x[bb], dtype=np.float32).reshape(-1, 128, D)
        xr = np.concatenate([xr[:, 64 * parity:, :], xr[:, :64 * parity, :]],
                            axis=1).reshape(-1, D)
        in_maps.append({
            "xs": np.ascontiguousarray(xr),
            "xq": np.ascontiguousarray(x[bb][gmap], dtype=np.float32),
            "masks": build_masks(cfg, parity),
            "bvq": bq, "bvk": bk, "bvv": bv_,
            "wt": wt, "ag": ag,
            "b2a": np.ascontiguousarray(B2[0:112], dtype=np.float32),
            "b2b": np.ascontiguousarray(B2[112:224], dtype=np.float32),
        })
    return in_maps, gmaps



class Sem:
    def __init__(self, handle):
        self.h = handle
        self.n = 0

    def inc(self, instr, k):
        instr.then_inc(self.h, k)
        self.n += k
        return instr

    def wait(self, eng, val=None):
        v = self.n if val is None else val
        if v > 0:
            eng.wait_ge(self.h, v)


def build_program(cfg):
    T, D, H = cfg["T"], cfg["D"], cfg["H"]
    HC, CPH, HD = cfg["HC"], cfg["CPH"], cfg["HD"]
    NSEQ, TH, NJ = cfg["NSEQ"], cfg["TH"], cfg["NJ"]
    GS, NGRP, M, NMASK = cfg["GS"], cfg["NGRP"], cfg["M"], cfg["NMASK"]
    NE, NER, NARE = cfg["NE"], cfg["NER"], cfg["NARE"]
    OGW, NOG = cfg["OGW"], cfg["NOG"]
    NDH, DHW = cfg["NDH"], cfg["DHW"]
    NTOT = NSEQ + NJ
    NTB = NSEQ // 4
    HCG = HC // 4
    CPB = max(1, HC // 8)
    PSA_W = GS * 128
    NKmax = M + GS * NGRP
    NU = H * NGRP

    def NK(g):
        return M + GS * (g + 1)

    nc = bass.Bass("TRN2", target_bir_lowering=False)

    xs = nc.dram_tensor("xs", [T, D], F32, kind="ExternalInput").ap()
    xq = nc.dram_tensor("xq", [TH, D], F32, kind="ExternalInput").ap()
    masks_in = nc.dram_tensor("masks", [NMASK, 128, PSA_W], BF16, kind="ExternalInput").ap()
    bvq_in = nc.dram_tensor("bvq", [128, HC], F32, kind="ExternalInput").ap()
    bvk_in = nc.dram_tensor("bvk", [128, HC], F32, kind="ExternalInput").ap()
    bvv_in = nc.dram_tensor("bvv", [128, HC], F32, kind="ExternalInput").ap()
    wt_in = nc.dram_tensor("wt", [D, D], F32, kind="ExternalInput").ap()
    ag_in = nc.dram_tensor("ag", [D, NARE], F32, kind="ExternalInput").ap()
    b2a_in = nc.dram_tensor("b2a", [112, D], F32, kind="ExternalInput").ap()
    b2b_in = nc.dram_tensor("b2b", [112, D], F32, kind="ExternalInput").ap()
    out_hbm = nc.dram_tensor("out", [TH, D], F32, kind="ExternalOutput").ap()

    xh_sc = nc.dram_tensor("xh_sc", [T, D], BF16, kind="Internal").ap()
    xq_sc = nc.dram_tensor("xq_sc", [TH, D], BF16, kind="Internal").ap()
    a_sc = nc.dram_tensor("a_sc", [TH, D], BF16, kind="Internal").ap()
    vb_sc = nc.dram_tensor("vb_sc", [1, D], BF16, kind="Internal").ap()

    with contextlib.ExitStack() as OS:
        def sb(stack, name, shp, dt=F32):
            return stack.enter_context(nc.sbuf_tensor(name, shp, dt))

        def ps(stack, name, shp):
            return stack.enter_context(nc.psum_tensor(name, shp, F32))

        ident = sb(OS, "ident", [128, 128], BF16)
        g_all = sb(OS, "g_all", [128, HC], F32)
        epsv = sb(OS, "epsv", [128, 1], F32)
        onesrow = sb(OS, "onesrow", [1, 128], BF16)

        sem_names = [
            "s_x0", "s_x1", "s_x2", "s_x3", "s_x4", "s_x5", "s_x6", "s_x7",
            "s_msk", "s_qg", "s_yp0", "s_yp1", "s_ab0", "s_ab1",
            "s_vbf", "s_xw", "s_xw1", "s_yb0", "s_yb1", "s_yb2",
            "s_aw0", "s_aw1", "s_aw2", "s_aw3", "s_yw0", "s_yw1",
            "s_os0", "s_os1", "s_gW", "s_wt0", "s_wt1",
            "s_pe", "s_dve", "s_act", "s_pool",
        ]
        SS = {nm: Sem(OS.enter_context(nc.semaphore(nm))) for nm in sem_names}
        s_vtM = Sem(OS.enter_context(nc.semaphore("s_vtM")))
        s_ac = [Sem(OS.enter_context(nc.semaphore(f"s_ac{i}"))) for i in range(3)]
        s_gt = [Sem(OS.enter_context(nc.semaphore(f"s_gt{i}"))) for i in range(2)]
        s_xqw = Sem(OS.enter_context(nc.semaphore("s_xqw")))
        s_kt0 = Sem(OS.enter_context(nc.semaphore("s_kt0")))
        s_ktR = Sem(OS.enter_context(nc.semaphore("s_ktR")))
        s_vt0 = Sem(OS.enter_context(nc.semaphore("s_vt0")))
        s_vtR = Sem(OS.enter_context(nc.semaphore("s_vtR")))
        s_x = [SS[f"s_x{i}"] for i in range(8)]
        s_yp = [SS["s_yp0"], SS["s_yp1"]]
        s_ab = [SS["s_ab0"], SS["s_ab1"]]
        s_aw = [SS["s_aw0"], SS["s_aw1"], SS["s_aw2"], SS["s_aw3"]]
        s_yw = [SS["s_yw0"], SS["s_yw1"]]
        s_os = [SS["s_os0"], SS["s_os1"]]
        s_wt = [SS["s_wt0"], SS["s_wt1"]]
        s_msk, s_qg, s_xw = SS["s_msk"], SS["s_qg"], SS["s_xw"]
        s_xwp = [SS["s_xw"], SS["s_xw1"]]
        s_yb = [SS["s_yb0"], SS["s_yb1"], SS["s_yb2"]]
        s_gW, s_vbf = SS["s_gW"], SS["s_vbf"]
        pe, dve, act, pool = SS["s_pe"], SS["s_dve"], SS["s_act"], SS["s_pool"]

        block = OS.enter_context(nc.Block())

        with contextlib.ExitStack() as P0:
            iota0 = sb(P0, "iota0", [128, 128], F32)
            ioh = sb(P0, "ioh", [4, HC], F32)
            oh_a = sb(P0, "oh_a", [4, HC], F32)
            oh_b = sb(P0, "oh_b", [4, HC], F32)
            onehot = sb(P0, "onehot", [4, HC], BF16)
            ones4 = sb(P0, "ones4", [4, 128], F32)
            onescol = sb(P0, "onescol", [128, 1], BF16)
            bvqf = sb(P0, "bvqf", [128, HC], F32)
            bvkf = sb(P0, "bvkf", [128, HC], F32)
            bvvf = sb(P0, "bvvf", [128, HC], F32)
            absb = sb(P0, "absb", [128, 12], BF16)
            absr = sb(P0, "absr", [128, 12], F32)
            alphas = sb(P0, "alphas", [4, 3], F32)
            alrep = sb(P0, "alrep", [4, 128 * 3], BF16)
            abq = sb(P0, "abq", [128, HC], F32)
            abk = sb(P0, "abk", [128, HC], F32)
            abv = sb(P0, "abv", [128, HC], F32)
            sgn = sb(P0, "sgn", [128, HC], F32)
            zro = sb(P0, "zro", [128, HC], F32)
            bndq = sb(P0, "bndq", [128, HC], F32)
            bndk = sb(P0, "bndk", [128, HC], F32)
            bndv = sb(P0, "bndv", [128, HC], BF16)
            vbtmp = sb(P0, "vbtmp", [HC, 128], BF16)
            p0a = ps(P0, "p0a", [128, 512])
            p0b = ps(P0, "p0b", [128, 512])

            def vstep(v, ins):
                dve.inc(ins, 1)
                v.wait_ge(dve.h, dve.n)

            @block.gpsimd
            def _(gp):
                gp.iota(iota0[:], [[1, 128]], base=0, channel_multiplier=-1,
                        allow_small_or_imprecise_dtypes=True)
                gp.iota(ioh[:], [[1, HC]], base=0, channel_multiplier=-CPH,
                        allow_small_or_imprecise_dtypes=True)
                gp.memset(epsv[:], 1e-5)
                gp.memset(ones4[:], 1.0)
                gp.memset(onescol[:], 1.0)
                pool.inc(gp.memset(onesrow[:], 1.0), 1)

            @block.sync
            def _(sy):
                s_msk.inc(sy.dma_start(out=bvqf[:], in_=bvq_in), 16)
                s_msk.inc(sy.dma_start(out=bvkf[:], in_=bvk_in), 16)
                s_msk.inc(sy.dma_start(out=bvvf[:], in_=bvv_in), 16)

            @block.vector
            def _(v):
                pool.wait(v)
                s_msk.wait(v)
                vstep(v, v.tensor_scalar(ident[:], iota0[:], 0.0, None, op0=ALU.is_equal))
                vstep(v, v.tensor_scalar(oh_a[:], ioh[:], -0.5, None, op0=ALU.is_ge))
                vstep(v, v.tensor_scalar(oh_b[:], ioh[:], CPH - 0.5, None, op0=ALU.is_le))
                vstep(v, v.tensor_tensor(onehot[:], oh_a[:], oh_b[:], op=ALU.mult))
                for t, bvf in enumerate([bvqf, bvkf, bvvf]):
                    for h in range(4):
                        vstep(v, v.tensor_reduce(
                            absr[:, t * 4 + h:t * 4 + h + 1],
                            bvf[:, h * CPH:(h + 1) * CPH], axis=AXX, op=ALU.add,
                            apply_absolute_value=True))
                vstep(v, v.tensor_copy(absb[:], absr[:]))

            @block.tensor
            def _(t):
                t.wait_ge(dve.h, dve.n)
                for tt in range(3):
                    ins = t.matmul(p0a[0:4, tt:tt + 1], absb[:, tt * 4:(tt + 1) * 4],
                                   onescol[:], start=True, stop=True)
                pe.inc(ins, 1)

            @block.vector
            def _(v):
                v.wait_ge(pe.h, pe.n)
                vstep(v, v.tensor_scalar(alphas[:], p0a[0:4, 0:3], 1.0 / HD, None,
                                         op0=ALU.mult))
                for t in range(3):
                    vstep(v, v.tensor_scalar(alrep[:, t * 128:(t + 1) * 128], ones4[:],
                                             alphas[:, t:t + 1], None, op0=ALU.mult))

            @block.tensor
            def _(t):
                t.wait_ge(dve.h, dve.n)
                pe.inc(t.matmul(p0a[:, 0:HC], alrep[:, 0:128], onehot[:],
                                start=True, stop=True), 1)
                pe.inc(t.matmul(p0b[:, 0:HC], alrep[:, 128:256], onehot[:],
                                start=True, stop=True), 1)

            @block.vector
            def _(v):
                v.wait_ge(pe.h, pe.n)
                vstep(v, v.tensor_copy(abq[:], p0a[:, 0:HC]))
                vstep(v, v.tensor_copy(abk[:], p0b[:, 0:HC]))

            @block.tensor
            def _(t):
                t.wait_ge(dve.h, dve.n)
                pe.inc(t.matmul(p0a[:, 0:HC], alrep[:, 256:384], onehot[:],
                                start=True, stop=True), 1)

            @block.vector
            def _(v):
                v.wait_ge(pe.h, pe.n)
                vstep(v, v.tensor_copy(abv[:], p0a[:, 0:HC]))

            for bvf, ab, out_bind, last in [(bvqf, abq, bndq, False),
                                            (bvkf, abk, bndk, True),
                                            (bvvf, abv, bndv, None)]:
                @block.scalar
                def _(s, bvf=bvf):
                    s.wait_ge(dve.h, dve.n)
                    act.inc(s.activation(sgn[:], bvf[:], AF.Sign), 1)

                @block.vector
                def _(v, bvf=bvf, ab=ab, out_bind=out_bind, last=last):
                    v.wait_ge(act.h, act.n)
                    vstep(v, v.tensor_scalar(zro[:], bvf[:], 0.0, None, op0=ALU.is_equal))
                    vstep(v, v.tensor_tensor(sgn[:], sgn[:], zro[:], op=ALU.add))
                    vstep(v, v.tensor_tensor(out_bind[:], sgn[:], ab[:], op=ALU.mult))
                    if last is True:
                        vstep(v, v.scalar_tensor_tensor(g_all[:], bndq[:], 0.125,
                                                        bndk[:], op0=ALU.mult,
                                                        op1=ALU.mult))

            @block.tensor
            def _(t):
                t.wait_ge(dve.h, dve.n)
                pe.inc(t.matmul(p0b[0:HC, 0:128], bndv[:], ident[:],
                                start=True, stop=True), 1)

            @block.scalar
            def _(s):
                s.wait_ge(pe.h, pe.n)
                act.inc(s.activation(vbtmp[:], p0b[0:HC, 0:128], AF.Copy), 1)

            @block.gpsimd
            def _(gp):
                gp.wait_ge(act.h, act.n)
                s_gW.inc(gp.dma_start(
                    out=vb_sc.rearrange("o (c p) -> (o c) p", p=128), in_=vbtmp[:]), 16)

        with contextlib.ExitStack() as P1:
            xbuf = [sb(P1, f"xbuf{i}", [128, D], F32) for i in range(8)]
            xhb = [sb(P1, f"xhb{i}", [128, D], BF16) for i in range(2)]
            st6 = [sb(P1, f"st6_{i}", [128, 6 * (D // 512)], F32) for i in range(2)]
            mv4 = [sb(P1, f"mv4_{i}", [128, 8], F32) for i in range(2)]
            nmn4 = [sb(P1, f"nmn4_{i}", [128, 4], F32) for i in range(2)]
            lnv4 = [sb(P1, f"lnv4_{i}", [128, 4], F32) for i in range(2)]
            rsd4 = [sb(P1, f"rsd4_{i}", [128, 4], F32) for i in range(2)]
            nmr4 = [sb(P1, f"nmr4_{i}", [128, 4], F32) for i in range(2)]
            NCH = D // 512
            assert NSEQ % 4 == 0
            NGR = NSEQ // 4

            d1 = dve.n
            a1 = act.n
            x0 = [s.n for s in s_x]
            w0 = [s_xwp[0].n, s_xwp[1].n]
            msk0 = s_msk.n

            def BNE(i):
                return d1 + 10 * (i // 4) + 2 * (i % 4) + 1

            def NORME(i):
                return a1 + 6 * (i // 4) + 3 + (i % 4)

            def xload(eng, i):
                if i >= 8:
                    eng.wait_ge(dve.h, BNE(i - 8))
                    eng.wait_ge(act.h, NORME(i - 8))
                bix = i % 8
                s_x[bix].inc(eng.dma_start(out=xbuf[bix][:],
                                           in_=xs[i * 128:(i + 1) * 128, :]), 16)

            def xwrite(eng, i):
                eng.wait_ge(act.h, NORME(i))
                eng.wait_ge(s_xwp[i % 2].h, w0[i % 2] + 16 * (i // 2))
                s_xwp[i % 2].inc(eng.dma_start(
                    out=xh_sc[i * 128:(i + 1) * 128, :], in_=xhb[i % 2][:]), 16)

            @block.sync
            def _(sy):
                pe.wait(sy)
                act.wait(sy)
                dve.wait(sy)
                s_gW.wait(sy)
                for i in range(0, NSEQ, 2):
                    xload(sy, i)
                    if i >= 8:
                        xwrite(sy, i - 8)
                for i in range(max(0, NSEQ - 8), NSEQ, 2):
                    xwrite(sy, i)

            @block.gpsimd
            def _(gp):
                pe.wait(gp)
                act.wait(gp)
                dve.wait(gp)
                s_gW.wait(gp)
                gp.wait_ge(s_msk.h, msk0)
                for qg in range(NJ // 4):
                    t_need = M + 4 * qg + 3
                    gp.wait_ge(s_xwp[0].h, w0[0] + 16 * (t_need // 2 + 1))
                    gp.wait_ge(s_xwp[1].h, w0[1] + 16 * ((t_need + 1) // 2))
                    oview = xq_sc[4 * qg * 128:(4 * qg + 4) * 128, :].rearrange(
                        "(j r) d -> j r d", r=128)
                    ivA = xh_sc[4 * qg * 128:(4 * qg + 4) * 128, :].rearrange(
                        "(j r) d -> j r d", r=128)
                    ivB = xh_sc[(M + 4 * qg) * 128:(M + 4 * qg + 4) * 128, :].rearrange(
                        "(j r) d -> j r d", r=128)
                    s_ab[qg % 2].inc(gp.dma_start(
                        out=oview[:, 0:64, :], in_=ivA[:, 0:64, :]), 16)
                    s_ab[qg % 2].inc(gp.dma_start(
                        out=oview[:, 64:128, :], in_=ivB[:, 0:64, :]), 16)

            @block.vector
            def _(v):
                for g in range(NGR):
                    for t in range(4):
                        i = 4 * g + t
                        bix = i % 8
                        v.wait_ge(s_x[bix].h, x0[bix] + 16 * (i // 8 + 1))
                        if g >= 1 and t == 0:
                            v.wait_ge(act.h, a1 + 6 * (g - 1) + 2)
                        for ch in range(NCH):
                            ins = v.bn_stats(st6[i % 2][:, ch * 6:(ch + 1) * 6],
                                             xbuf[bix][:, ch * 512:(ch + 1) * 512])
                        dve.inc(ins, 1)
                        v.wait_ge(dve.h, d1 + 10 * g + 2 * t + 1)
                        dve.inc(v.bn_aggr(mv4[g % 2][:, 2 * t:2 * t + 2],
                                          st6[i % 2][:]), 1)
                    v.wait_ge(dve.h, dve.n)
                    mvv = mv4[g % 2][:].rearrange("p (t two) -> p t two", two=2)
                    dve.inc(v.tensor_scalar(nmn4[g % 2][:], mvv[:, :, 0], -1.0,
                                            None, op0=ALU.mult), 1)
                    v.wait_ge(dve.h, dve.n)
                    v.wait_ge(act.h, a1 + 6 * g + 2)
                    dve.inc(v.tensor_tensor(nmr4[g % 2][:], nmn4[g % 2][:],
                                            rsd4[g % 2][:], op=ALU.mult), 1)

            @block.scalar
            def _(s):
                pe.wait(s)
                s.wait_ge(dve.h, d1)
                s_gW.wait(s)
                for i in range(1, min(8, NSEQ), 2):
                    xload(s, i)
                for g in range(NGR):
                    mvv = mv4[g % 2][:].rearrange("p (t two) -> p t two", two=2)
                    s.wait_ge(dve.h, d1 + 10 * g + 8)
                    act.inc(s.activation(lnv4[g % 2][:], mvv[:, :, 1], AF.Ln,
                                         bias=epsv[:], scale=1.0), 1)
                    s.wait_ge(act.h, act.n)
                    act.inc(s.activation(rsd4[g % 2][:], lnv4[g % 2][:], AF.Exp,
                                         scale=-0.5), 1)
                    s.wait_ge(dve.h, d1 + 10 * g + 10)
                    for t in range(4):
                        i = 4 * g + t
                        p = i % 2
                        s.wait_ge(s_xwp[p].h, w0[p] + 16 * (i // 2))
                        act.inc(s.activation(xhb[p][:], xbuf[i % 8][:], AF.Identity,
                                             bias=nmr4[g % 2][:, t:t + 1],
                                             scale=rsd4[g % 2][:, t:t + 1]), 1)
                        if p == 1:
                            s.wait_ge(act.h, act.n)
                            s_xwp[1].inc(s.dma_start(
                                out=xh_sc[i * 128:(i + 1) * 128, :],
                                in_=xhb[1][:]), 16)
                    if 4 * (g + 2) < NSEQ:
                        for i in (4 * (g + 2) + 1, 4 * (g + 2) + 3):
                            xload(s, i)

            assert dve.n == d1 + 10 * NGR
            assert act.n == a1 + 6 * NGR

        with contextlib.ExitStack() as P2:
            maskt = sb(P2, "maskt", [128, NMASK * PSA_W], BF16)
            vbflat = sb(P2, "vbflat", [1, D], BF16)
            vb_full = sb(P2, "vb_full", [128, D], BF16)
            Kh = sb(P2, "Kh", [128, CPH * T], BF16)
            Vh = sb(P2, "Vh", [128, NSEQ * HD], BF16)
            attnT = sb(P2, "attnT", [128, NKmax * PSA_W], BF16)
            qgb = sb(P2, "qgb", [128, CPH * PSA_W], BF16)
            astg = [sb(P2, f"astg{i}", [128, HD], BF16) for i in range(4)]
            pOut = [ps(P2, f"pOut{i}", [128, max(1024, HD)]) for i in range(4)]

            pe2 = pe.n
            act2 = act.n
            dve2 = dve.n
            peVB = pe.n
            actVB = act.n
            qg0 = s_qg.n
            xw_all = [s_xwp[0].n, s_xwp[1].n]
            kt00 = s_kt0.n
            ktR0 = s_ktR.n
            aw0 = [s.n for s in s_aw]

            NVB = D // 512

            def peU(u):
                return pe2 + NVB + sum(NK(uu % NGRP) + 1 for uu in range(u))

            def actU(u):
                return act2 + NVB + sum(NK(uu % NGRP) + 4 for uu in range(u))

            def PASSA(u, k):
                return peU(u) + k + 1

            def PASSB(u):
                return peU(u) + NK(u % NGRP) + 1

            def SIG(u, k):
                return actU(u) + k + 1

            def EVAC(u, tt):
                return actU(u) + NK(u % NGRP) + tt + 1

            @block.sync
            def _(sy):
                sy.wait_ge(act.h, act2)
                sy.wait_ge(dve.h, dve2)
                sy.wait_ge(pe.h, pe2)
                s_gW.wait(sy)
                sy.wait_ge(s_xwp[0].h, xw_all[0])
                sy.wait_ge(s_xwp[1].h, xw_all[1])
                sy.wait_ge(s_ab[0].h, s_ab[0].n)
                sy.wait_ge(s_ab[1].h, s_ab[1].n)
                s_vbf.inc(sy.dma_start(out=vbflat[:], in_=vb_sc), 16)
                s_msk.inc(sy.dma_start(
                    out=maskt[:].rearrange("p (m t) -> p m t", m=NMASK),
                    in_=masks_in.rearrange("m p t -> p m t")), 16)
                for u in range(NU):
                    h, g = u // NGRP, u % NGRP
                    if u > 0:
                        sy.wait_ge(pe.h, PASSA(u - 1, NK((u - 1) % NGRP) - 1))
                    if g == 0 and h == 0:
                        sy.wait_ge(s_xwp[0].h, xw_all[0])
                        sy.wait_ge(s_xwp[1].h, xw_all[1])
                    for c in range(CPH):
                        s_qg.inc(sy.dma_start(
                            out=qgb[:, c * PSA_W:(c + 1) * PSA_W],
                            in_=xq_sc[g * PSA_W:(g + 1) * PSA_W,
                                      (h * CPH + c) * 128:(h * CPH + c + 1) * 128],
                            transpose=True), 16)
                    if g == 0:
                        for c in range(CPH):
                            s_kt0.inc(sy.dma_start(
                                out=Kh[:, c * T: c * T + 512],
                                in_=xh_sc[0:512,
                                          (h * CPH + c) * 128:(h * CPH + c + 1) * 128],
                                transpose=True), 16)
                        for c in range(CPH):
                            s_ktR.inc(sy.dma_start(
                                out=Kh[:, c * T + 512: (c + 1) * T],
                                in_=xh_sc[512:T,
                                          (h * CPH + c) * 128:(h * CPH + c + 1) * 128],
                                transpose=True), 16)

            qgdone = {}
            vbdone = {}
            VSTAGES = []
            for si, (kb0, kb1) in enumerate([(0, 4), (4, min(16, NSEQ)), (16, NSEQ)]):
                if kb1 > kb0:
                    VSTAGES.append((kb0, kb1, [s_vt0, s_vtM, s_vtR][si]))
            vst0 = {id(sem): sem.n for (_, _, sem) in VSTAGES}

            @block.vector
            def _(v):
                v.wait_ge(act.h, actVB + NVB)
                for u in range(NU):
                    h, g = u // NGRP, u % NGRP
                    v.wait_ge(s_qg.h, qg0 + 16 * CPH * (u + 1))
                    for c in range(CPH):
                        ins = v.tensor_scalar(qgb[:, c * PSA_W:(c + 1) * PSA_W],
                                              qgb[:, c * PSA_W:(c + 1) * PSA_W],
                                              g_all[:, h * CPH + c:h * CPH + c + 1],
                                              None, op0=ALU.mult)
                    dve.inc(ins, 1)
                    qgdone[u] = dve.n
                    if g == 0:
                        for si, (kb0, kb1, sem) in enumerate(VSTAGES):
                            v.wait_ge(sem.h, vst0[id(sem)] + 16 * (h + 1))
                            for kt in range(kb0, kb1):
                                off = kt * HD
                                ins = v.tensor_tensor(
                                    Vh[:, off:off + HD], Vh[:, off:off + HD],
                                    vb_full[:, h * HD:(h + 1) * HD], op=ALU.mult)
                            dve.inc(ins, 1)
                            vbdone[(h, kb0)] = dve.n

            @block.tensor
            def _(t):
                s_msk.wait(t)
                s_vbf.wait(t)
                for o in range(NVB):
                    if o >= 2:
                        t.wait_ge(act.h, actVB + (o - 2) + 1)
                    pe.inc(t.matmul(pOut[o % 2][:, 0:512], onesrow[:],
                                    vbflat[:, o * 512:(o + 1) * 512],
                                    start=True, stop=True), 1)
                for u in range(NU):
                    h, g = u // NGRP, u % NGRP
                    nk = NK(g)
                    t.wait_ge(dve.h, qgdone[u])
                    for k in range(nk):
                        midx = k - GS * g
                        compact = GS <= midx < M
                        if k == 0:
                            t.wait_ge(s_kt0.h, kt00 + 16 * CPH * (h + 1))
                        elif k == 4:
                            t.wait_ge(s_ktR.h, ktR0 + 16 * CPH * (h + 1))
                        if k < 4:
                            if u > 0:
                                t.wait_ge(act.h, EVAC(u - 1, 0 if k < 2 else 1))
                            else:
                                t.wait_ge(act.h, actVB + NVB)
                        else:
                            t.wait_ge(act.h, SIG(u, k - 4))
                        has_mask = 0 <= midx < GS or M <= midx < M + GS
                        mdix = midx if midx < GS else GS + (midx - M)
                        if compact:
                            psA = pOut[(k % 4) // 2][:, (k % 2) * PSA_W:(k % 2) * PSA_W + 256]
                            for c in range(CPH):
                                qgu = qgb[:, c * PSA_W:(c + 1) * PSA_W].rearrange(
                                    "p (j q) -> p j q", q=128)[:, :, 64:128]
                                ins = t.matmul(psA,
                                               Kh[:, c * T + k * 128: c * T + (k + 1) * 128],
                                               qgu, start=(c == 0), stop=(c == CPH - 1))
                        else:
                            psA = pOut[(k % 4) // 2][:, (k % 2) * PSA_W:(k % 2 + 1) * PSA_W]
                            for c in range(CPH):
                                ins = t.matmul(psA,
                                               Kh[:, c * T + k * 128: c * T + (k + 1) * 128],
                                               qgb[:, c * PSA_W:(c + 1) * PSA_W],
                                               start=(c == 0),
                                               stop=(c == CPH - 1 and not has_mask))
                            if has_mask:
                                ins = t.matmul(psA, ident[:],
                                               maskt[:, mdix * PSA_W:(mdix + 1) * PSA_W],
                                               start=False, stop=True)
                        pe.inc(ins, 1)
                    t.wait_ge(act.h, SIG(u, nk - 1))
                    for k in range(nk):
                        midx = k - GS * g
                        compact = GS <= midx < M
                        if (h, k) in vbdone:
                            t.wait_ge(dve.h, vbdone[(h, k)])
                        for tt in range(GS):
                            for dh in range(NDH):
                                if compact:
                                    ins = t.matmul(
                                        pOut[tt][64:128, dh * DHW:(dh + 1) * DHW],
                                        attnT[:, k * PSA_W + tt * 64: k * PSA_W + (tt + 1) * 64],
                                        Vh[:, k * HD + dh * DHW: k * HD + (dh + 1) * DHW],
                                        start=False, stop=(k == nk - 1))
                                else:
                                    ins = t.matmul(
                                        pOut[tt][:, dh * DHW:(dh + 1) * DHW],
                                        attnT[:, k * PSA_W + tt * 128: k * PSA_W + (tt + 1) * 128],
                                        Vh[:, k * HD + dh * DHW: k * HD + (dh + 1) * DHW],
                                        start=(k == 0), stop=(k == nk - 1))
                    pe.inc(ins, 1)

            aw_after = {}

            @block.scalar
            def _(s):
                for o in range(NVB):
                    s.wait_ge(pe.h, peVB + o + 1)
                    act.inc(s.activation(vb_full[:, o * 512:(o + 1) * 512],
                                         pOut[o % 2][:, 0:512], AF.Copy), 1)
                for u in range(NU):
                    h, g = u // NGRP, u % NGRP
                    nk = NK(g)
                    if g == 0:
                        if h > 0:
                            s.wait_ge(pe.h, peU(u))
                        else:
                            s.wait_ge(s_xwp[0].h, xw_all[0])
                            s.wait_ge(s_xwp[1].h, xw_all[1])
                        for (kb0, kb1, sem) in VSTAGES:
                            sem.inc(s.dma_start(
                                out=Vh[:, kb0 * HD:kb1 * HD].rearrange(
                                    "p (k d) -> p k d", k=kb1 - kb0),
                                in_=xh_sc[kb0 * 128:kb1 * 128,
                                          h * HD:(h + 1) * HD].rearrange(
                                    "(k p) d -> p k d", p=128)), 16)
                    for k in range(nk):
                        midx = k - GS * g
                        w = 256 if GS <= midx < M else PSA_W
                        s.wait_ge(pe.h, PASSA(u, k))
                        act.inc(s.activation(
                            attnT[:, k * PSA_W:k * PSA_W + w],
                            pOut[(k % 4) // 2][:, (k % 2) * PSA_W:(k % 2) * PSA_W + w],
                            AF.Sigmoid), 1)
                    s.wait_ge(pe.h, PASSB(u))
                    if u > 0:
                        gp_prev = (u - 1) % NGRP
                        s.wait_ge(s_aw[gp_prev].h, aw_after[u - 1])
                    for tt in range(GS):
                        act.inc(s.activation(astg[tt][:], pOut[tt][:, 0:HD], AF.Copy), 1)
                        s.wait_ge(act.h, act.n)
                        row = g * PSA_W + tt * 128
                        s_aw[g].inc(s.dma_start(
                            out=a_sc[row:row + 128, h * HD:(h + 1) * HD],
                            in_=astg[tt][:]), 16)
                    aw_after[u] = s_aw[g].n

            assert pe.n == peU(NU), (pe.n, peU(NU))
            assert act.n == actU(NU), (act.n, actU(NU))

        with contextlib.ExitStack() as P3S:
            yfm = sb(P3S, "yfm", [128, HC * TH], BF16)
            hgTa = sb(P3S, "hgTa", [112, TH], BF16)
            hgTb = sb(P3S, "hgTb", [112, TH], BF16)

            with contextlib.ExitStack() as PP:
                ybuf = [sb(PP, f"ybuf{i}", [128, D], F32) for i in range(2)]
                yhb = [sb(PP, f"yhb{i}", [128, D], BF16) for i in range(2)]
                abuf = [sb(PP, f"abuf{i}", [128, D], BF16) for i in range(2)]
                st6p = [sb(PP, f"st6p{i}", [128, 6 * (D // 512)], F32) for i in range(2)]
                mvp = [sb(PP, f"mvp{i}", [128, 2], F32) for i in range(2)]
                nmnp = [sb(PP, f"nmnp{i}", [128, 1], F32) for i in range(2)]
                sqvp = [sb(PP, f"sqvp{i}", [128, 1], F32) for i in range(2)]
                rsdp = [sb(PP, f"rsdp{i}", [128, 1], F32) for i in range(2)]
                nmrp = [sb(PP, f"nmrp{i}", [128, 1], F32) for i in range(2)]
                pq = [ps(PP, f"pq{i}", [128, 512]) for i in range(8)]
                NCH = D // 512
                DSP = max(512, min(D - 512, (D * 5 // 8) // 512 * 512))

                pe3 = pe.n
                act3 = act.n
                dve3 = dve.n
                pool3 = pool.n
                yw0v = [s_yw[0].n, s_yw[1].n]
                awF = [s.n for s in s_aw]
                yb0v = [s.n for s in s_yb]
                ac0v = [s.n for s in s_ac]

                DVE_PJ, ACT_PJ, PE_PJ = 3, 13, 2

                def ADDS(j):
                    return dve3 + DVE_PJ * j + 1

                def AGGRP(j):
                    return dve3 + DVE_PJ * j + 3

                def NORMP(j):
                    return act3 + ACT_PJ * j + 5

                def EVA(j, r, q):
                    return act3 + ACT_PJ * j + 5 + r * 4 + q + 1

                def TRND(j, r):
                    return pe3 + PE_PJ * j + r + 1

                def ywcnt(j):
                    return yw0v[j % 2] + 16 * (j // 2 + 1)

                @block.sync
                def _(sy):
                    sy.wait_ge(pe.h, pe3)
                    sy.wait_ge(act.h, act3)
                    sy.wait_ge(dve.h, dve3)
                    for g4 in range(4):
                        sy.wait_ge(s_aw[g4].h, awF[g4])
                    for j in range(NJ):
                        p = j % 2
                        if j >= 2:
                            sy.wait_ge(dve.h, AGGRP(j - 2))
                            sy.wait_ge(act.h, NORMP(j - 2))
                            sy.wait_ge(s_yw[p].h, ywcnt(j - 2))
                        s_yb[p].inc(sy.dma_start(
                            out=ybuf[p][:], in_=xq[j * 128:(j + 1) * 128, :]), 16)

                @block.gpsimd
                def _(gp):
                    gp.wait_ge(pe.h, pe3)
                    gp.wait_ge(act.h, act3)
                    for g4 in range(4):
                        gp.wait_ge(s_aw[g4].h, awF[g4])
                    def a_load(jj):
                        pj = jj % 2
                        if jj >= 2:
                            gp.wait_ge(dve.h, ADDS(jj - 2))
                            gp.wait_ge(pool.h, pool3 + jj - 1)
                        s_ac[pj].inc(gp.dma_start(
                            out=abuf[pj][:], in_=a_sc[jj * 128:(jj + 1) * 128, :]), 16)
                    a_load(0)
                    for j in range(NJ):
                        p = j % 2
                        if j + 1 < NJ:
                            a_load(j + 1)
                        gp.wait_ge(s_yb[p].h, yb0v[p] + 16 * (j // 2 + 1))
                        gp.wait_ge(s_ac[p].h, ac0v[p] + 16 * (j // 2 + 1))
                        pool.inc(gp.tensor_tensor(ybuf[p][:, DSP:D],
                                                  ybuf[p][:, DSP:D],
                                                  abuf[p][:, DSP:D], op=ALU.add), 1)

                @block.vector
                def _(v):
                    for j in range(NJ):
                        p = j % 2
                        v.wait_ge(s_yb[p].h, yb0v[p] + 16 * (j // 2 + 1))
                        v.wait_ge(s_ac[p].h, ac0v[p] + 16 * (j // 2 + 1))
                        dve.inc(v.tensor_tensor(ybuf[p][:, 0:DSP], ybuf[p][:, 0:DSP],
                                                abuf[p][:, 0:DSP], op=ALU.add), 1)
                        v.wait_ge(dve.h, ADDS(j))
                        v.wait_ge(pool.h, pool3 + j + 1)
                        for ch in range(NCH):
                            ins = v.bn_stats(st6p[j % 2][:, ch * 6:(ch + 1) * 6],
                                             ybuf[p][:, ch * 512:(ch + 1) * 512])
                        dve.inc(ins, 1)
                        v.wait_ge(dve.h, dve.n)
                        if j >= 2:
                            v.wait_ge(act.h, act3 + ACT_PJ * (j - 2) + 2)
                        dve.inc(v.bn_aggr(mvp[j % 2][:], st6p[j % 2][:]), 1)

                @block.scalar
                def _(s):
                    for j in range(NJ):
                        p = j % 2
                        s.wait_ge(dve.h, AGGRP(j))
                        act.inc(s.activation(nmnp[p][:], mvp[p][:, 0:1], AF.Identity,
                                             scale=-1.0), 1)
                        act.inc(s.activation(sqvp[p][:], mvp[p][:, 1:2], AF.Ln,
                                             bias=epsv[:], scale=1.0), 1)
                        s.wait_ge(act.h, act.n)
                        act.inc(s.activation(rsdp[p][:], sqvp[p][:], AF.Exp,
                                             scale=-0.5), 1)
                        s.wait_ge(act.h, act.n)
                        act.inc(s.activation(nmrp[p][:], nmnp[p][:], AF.Identity,
                                             scale=rsdp[p][:]), 1)
                        s.wait_ge(act.h, act.n)
                        if j >= 2:
                            s.wait_ge(pe.h, TRND(j - 2, 1))
                        act.inc(s.activation(yhb[j % 2][:], ybuf[p][:], AF.Identity,
                                             bias=nmrp[p][:], scale=rsdp[p][:]), 1)
                        s.wait_ge(act.h, act.n)
                        s.wait_ge(s_yw[p].h, yw0v[p] + 16 * (j // 2))
                        s_yw[p].inc(s.dma_start(out=out_hbm[j * 128:(j + 1) * 128, :],
                                                in_=ybuf[p][:]), 16)
                        for r in range(2):
                            s.wait_ge(pe.h, TRND(j, r))
                            for q in range(4):
                                cbase = r * (HC // 2) + q * CPB
                                act.inc(s.activation(
                                    yfm[:].rearrange("p (c t) -> p c t", t=TH)[
                                        :, cbase:cbase + CPB, j * 128:(j + 1) * 128],
                                    pq[r * 4 + q][:, 0:CPB * 128].rearrange(
                                        "p (c t) -> p c t", t=128),
                                    AF.Copy), 1)

                @block.tensor
                def _(t):
                    for j in range(NJ):
                        t.wait_ge(act.h, NORMP(j))
                        for r in range(2):
                            if j >= 1:
                                t.wait_ge(act.h, EVA(j - 1, r, 3))
                            for ci in range(HC // 2):
                                c = r * (HC // 2) + ci
                                q, col = ci // CPB, ci % CPB
                                ins = t.matmul(
                                    pq[r * 4 + q][:, col * 128:(col + 1) * 128],
                                    yhb[j % 2][:, c * 128:(c + 1) * 128],
                                    ident[:], start=True, stop=True)
                            pe.inc(ins, 1)

                assert dve.n == dve3 + DVE_PJ * NJ
                assert act.n == act3 + ACT_PJ * NJ
                assert pe.n == pe3 + PE_PJ * NJ

            with contextlib.ExitStack() as PG:
                RANK = cfg["RANK"]
                agb = sb(PG, "agb", [128, HC * NARE], BF16)
                lgt_all = sb(PG, "lgt_all", [128, NJ * NARE], F32)
                m1a = sb(PG, "m1a", [128, NJ], F32)
                m2a = sb(PG, "m2a", [128, NJ], F32)
                d12a = sb(PG, "d12a", [128, NJ], F32)
                p1a = sb(PG, "p1a", [128, NJ], F32)
                p2a = sb(PG, "p2a", [128, NJ], F32)
                km1a = sb(PG, "km1a", [128, NJ * NE], F32)
                km2a = sb(PG, "km2a", [128, NJ * NE], F32)
                l2a = sb(PG, "l2a", [128, NJ * NE], F32)
                g28a = sb(PG, "g28a", [128, NJ * NE], F32)
                hgball = sb(PG, "hgball", [128, NJ * NER], BF16)
                psL = [ps(PG, "psL0", [128, 512]), ps(PG, "psL1", [128, 512])]
                psT = [ps(PG, "psT0", [128, 512]), ps(PG, "psT1", [128, 512])]

                peG = pe.n
                actG = act.n
                dveG = dve.n
                gw0 = s_gW.n

                def LMM(j):
                    return peG + j + 1

                def TMM(j):
                    return peG + NJ + j + 1

                def LG(j):
                    return dveG + j + 1

                D_L = dveG + NJ
                D_M1 = D_L + 1
                D_KM1 = D_M1 + NJ
                D_L2 = D_KM1 + NJ
                D_M2 = D_L2 + 1
                D_KM2 = D_M2 + NJ
                D_D12 = D_KM2 + 1
                D_G28a = D_D12 + NJ
                D_G28b = D_G28a + NJ
                D_SC = D_G28b + 1
                D_HG = D_SC + RANK
                A_P = actG + 2

                def A_EV(j):
                    return A_P + 2 * (j + 1)

                @block.gpsimd
                def _(gp):
                    gp.wait_ge(act.h, actG)
                    gp.wait_ge(dve.h, dveG)
                    gp.wait_ge(pe.h, peG)
                    gp.wait_ge(s_yw[0].h, s_yw[0].n)
                    gp.wait_ge(s_yw[1].h, s_yw[1].n)
                    for sy3 in s_yb + s_ac:
                        gp.wait_ge(sy3.h, sy3.n)
                    s_gW.inc(gp.dma_start(
                        out=agb[:].rearrange("p (c e) -> p c e", e=NARE),
                        in_=ag_in.rearrange("(c p) e -> p c e", p=128)), 16)

                @block.tensor
                def _(t):
                    t.wait_ge(s_gW.h, gw0 + 16)
                    t.wait_ge(act.h, actG)
                    for j in range(NJ):
                        if j >= 2:
                            t.wait_ge(dve.h, LG(j - 2))
                        for c in range(HC):
                            ins = t.matmul(psL[j % 2][:, 0:NARE],
                                           yfm[:, c * TH + j * 128: c * TH + (j + 1) * 128],
                                           agb[:, c * NARE:(c + 1) * NARE],
                                           start=(c == 0), stop=(c == HC - 1))
                        pe.inc(ins, 1)
                    for j in range(NJ):
                        t.wait_ge(dve.h, D_HG)
                        if j >= 2:
                            t.wait_ge(act.h, A_EV(j - 2))
                        t.matmul(psT[j % 2][0:112, 0:128],
                                 hgball[:, j * NER:j * NER + 112], ident[:],
                                 start=True, stop=True)
                        pe.inc(t.matmul(psT[j % 2][0:112, 128:256],
                                        hgball[:, j * NER + 112:(j + 1) * NER], ident[:],
                                        start=True, stop=True), 1)

                @block.vector
                def _(v):
                    lga = lgt_all[:].rearrange("p (j e) -> p j e", e=NARE)
                    for j in range(NJ):
                        v.wait_ge(pe.h, LMM(j))
                        dve.inc(v.tensor_copy(lgt_all[:, j * NARE:(j + 1) * NARE],
                                              psL[j % 2][:, 0:NARE]), 1)
                    v.wait_ge(dve.h, D_L)
                    dve.inc(v.reduce_max(m1a[:], lga[:, :, 0:NE], axis=AXX), 1)
                    v.wait_ge(dve.h, D_M1)
                    for j in range(NJ):
                        dve.inc(v.tensor_scalar(km1a[:, j * NE:(j + 1) * NE],
                                                lgt_all[:, j * NARE:j * NARE + NE],
                                                m1a[:, j:j + 1], None, op0=ALU.is_ge), 1)
                    v.wait_ge(dve.h, D_KM1)
                    for j in range(NJ):
                        dve.inc(v.scalar_tensor_tensor(
                            l2a[:, j * NE:(j + 1) * NE], km1a[:, j * NE:(j + 1) * NE],
                            -BIGL, lgt_all[:, j * NARE:j * NARE + NE],
                            op0=ALU.mult, op1=ALU.add), 1)
                    v.wait_ge(dve.h, D_L2)
                    dve.inc(v.reduce_max(m2a[:], l2a[:].rearrange(
                        "p (j e) -> p j e", e=NE), axis=AXX), 1)
                    v.wait_ge(dve.h, D_M2)
                    for j in range(NJ):
                        dve.inc(v.tensor_scalar(km2a[:, j * NE:(j + 1) * NE],
                                                l2a[:, j * NE:(j + 1) * NE],
                                                m2a[:, j:j + 1], None, op0=ALU.is_ge), 1)
                    v.wait_ge(dve.h, D_KM2)
                    dve.inc(v.tensor_tensor(d12a[:], m1a[:], m2a[:], op=ALU.subtract), 1)
                    v.wait_ge(act.h, A_P)
                    for j in range(NJ):
                        dve.inc(v.tensor_scalar(g28a[:, j * NE:(j + 1) * NE],
                                                km1a[:, j * NE:(j + 1) * NE],
                                                p1a[:, j:j + 1], None, op0=ALU.mult), 1)
                    v.wait_ge(dve.h, D_G28a)
                    for j in range(NJ):
                        dve.inc(v.scalar_tensor_tensor(
                            g28a[:, j * NE:(j + 1) * NE], km2a[:, j * NE:(j + 1) * NE],
                            p2a[:, j:j + 1], g28a[:, j * NE:(j + 1) * NE],
                            op0=ALU.mult, op1=ALU.add), 1)
                    v.wait_ge(dve.h, D_G28b)
                    dve.inc(v.tensor_scalar(g28a[:], g28a[:], 2.0, None, op0=ALU.mult), 1)
                    v.wait_ge(dve.h, D_SC)
                    hgv = hgball[:].rearrange("p (j e r) -> p j e r", e=NE, r=RANK)
                    g28v = g28a[:].rearrange("p (j e) -> p j e", e=NE)
                    lgh = lga[:, :, NE:NARE].rearrange("p j (e r) -> p j e r", r=RANK)
                    for r8 in range(RANK):
                        dve.inc(v.tensor_tensor(
                            hgv[:, :, :, r8], lgh[:, :, :, r8],
                            g28v[:, :, :], op=ALU.mult), 1)

                @block.scalar
                def _(s):
                    s.wait_ge(dve.h, D_D12)
                    act.inc(s.activation(p1a[:], d12a[:], AF.Sigmoid, scale=1.0), 1)
                    act.inc(s.activation(p2a[:], d12a[:], AF.Sigmoid, scale=-1.0), 1)
                    for j in range(NJ):
                        s.wait_ge(pe.h, TMM(j))
                        act.inc(s.activation(hgTa[:, j * 128:(j + 1) * 128],
                                             psT[j % 2][0:112, 0:128], AF.Copy), 1)
                        act.inc(s.activation(hgTb[:, j * 128:(j + 1) * 128],
                                             psT[j % 2][0:112, 128:256], AF.Copy), 1)

                assert dve.n == D_HG
                assert act.n == A_P + 2 * NJ
                assert pe.n == peG + 2 * NJ

            with contextlib.ExitStack() as PM:
                wtb = [sb(PM, f"wtb{i}", [128, HC * OGW], BF16) for i in range(2)]
                b2ab = sb(PM, "b2ab", [112, D], BF16)
                b2bb = sb(PM, "b2bb", [112, D], BF16)
                ypc = [sb(PM, f"ypc{i}", [128, OGW], F32) for i in range(2)]
                ost = [sb(PM, f"ost{i}", [128, OGW], F32) for i in range(2)]
                pm = [ps(PM, f"pm{i}", [128, 512]) for i in range(4)]

                peM = pe.n
                dveM = dve.n
                actM = act.n
                gwM = s_gW.n
                yp0v = [s_yp[0].n, s_yp[1].n]
                os0v = [s_os[0].n, s_os[1].n]
                wt0v = [s_wt[0].n, s_wt[1].n]
                NID = NOG * NJ

                def MME(i):
                    return peM + i + 1

                def EVE(i):
                    return dveM + i + 1

                @block.gpsimd
                def _(gp):
                    gp.wait_ge(act.h, actM)
                    gp.wait_ge(dve.h, dveM)
                    gp.wait_ge(pe.h, peM)
                    s_gW.inc(gp.dma_start(out=b2ab[:], in_=b2a_in), 16)
                    s_gW.inc(gp.dma_start(out=b2bb[:], in_=b2b_in), 16)
                    for og in range(NOG):
                        if og >= 2:
                            gp.wait_ge(pe.h, MME((og - 1) * NJ - 1))
                        for q in range(4):
                            s_wt[og % 2].inc(gp.dma_start(
                                out=wtb[og % 2][:, q * HCG * OGW:(q + 1) * HCG * OGW
                                                ].rearrange("p (c o) -> p c o", o=OGW),
                                in_=wt_in[q * HCG * 128:(q + 1) * HCG * 128,
                                          og * OGW:(og + 1) * OGW
                                          ].rearrange("(c p) o -> p c o", p=128)), 16)

                @block.tensor
                def _(t):
                    t.wait_ge(s_gW.h, gwM + 32)
                    t.wait_ge(act.h, actM)
                    t.wait_ge(dve.h, dveM)
                    for og in range(NOG):
                        t.wait_ge(s_wt[og % 2].h, wt0v[og % 2] + 64 * (og // 2 + 1))
                        for tt in range(NJ):
                            i = og * NJ + tt
                            if i >= 4:
                                t.wait_ge(dve.h, EVE(i - 4))
                            bank = pm[tt % 4]
                            for c in range(HC):
                                t.matmul(bank[:, 0:OGW],
                                         yfm[:, c * TH + tt * 128: c * TH + (tt + 1) * 128],
                                         wtb[og % 2][:, c * OGW:(c + 1) * OGW],
                                         start=(c == 0), stop=False)
                            t.matmul(bank[:, 0:OGW],
                                     hgTa[:, tt * 128:(tt + 1) * 128],
                                     b2ab[:, og * OGW:(og + 1) * OGW],
                                     start=False, stop=False)
                            pe.inc(t.matmul(bank[:, 0:OGW],
                                            hgTb[:, tt * 128:(tt + 1) * 128],
                                            b2bb[:, og * OGW:(og + 1) * OGW],
                                            start=False, stop=True), 1)

                @block.sync
                def _(sy):
                    sy.wait_ge(s_yw[0].h, s_yw[0].n)
                    sy.wait_ge(s_yw[1].h, s_yw[1].n)
                    sy.wait_ge(act.h, actM)
                    sy.wait_ge(dve.h, dveM)
                    sy.wait_ge(pe.h, peM)
                    for i in range(NID):
                        og, tt = i // NJ, i % NJ
                        p = i % 2
                        if i >= 2:
                            sy.wait_ge(dve.h, EVE(i - 2))
                        s_yp[p].inc(sy.dma_start(
                            out=ypc[p][:],
                            in_=out_hbm[tt * 128:(tt + 1) * 128,
                                        og * OGW:(og + 1) * OGW]), 16)

                @block.vector
                def _(v):
                    for i in range(NID):
                        og, tt = i // NJ, i % NJ
                        p = i % 2
                        v.wait_ge(pe.h, MME(i))
                        v.wait_ge(s_yp[p].h, yp0v[p] + 16 * (i // 2 + 1))
                        if i >= 2:
                            v.wait_ge(s_os[p].h, os0v[p] + 16 * (i // 2))
                        dve.inc(v.tensor_tensor(ost[p][:], pm[tt % 4][:, 0:OGW],
                                                ypc[p][:], op=ALU.add), 1)

                @block.scalar
                def _(s):
                    for i in range(NID):
                        og, tt = i // NJ, i % NJ
                        p = i % 2
                        s.wait_ge(dve.h, EVE(i))
                        s_os[p].inc(s.dma_start(
                            out=out_hbm[tt * 128:(tt + 1) * 128,
                                        og * OGW:(og + 1) * OGW],
                            in_=ost[p][:]), 16)

                @block.sync
                def _(sy):
                    sy.wait_ge(s_os[0].h, s_os[0].n)
                    sy.wait_ge(s_os[1].h, s_os[1].n)

    return nc


_PROG_CACHE = {}


def _get_program(key, cfg):
    if key not in _PROG_CACHE:
        _PROG_CACHE[key] = build_program(cfg)
    return _PROG_CACHE[key]


def run(cfg, x, bvq, bvk, bvv, W, gateW, loraA, loraB, core_ids=None, sim=False,
        sim_cores=2):
    in_maps, gmaps = stage_inputs(cfg, x, bvq, bvk, bvv, W, gateW, loraA, loraB)
    nc = _get_program((cfg["T"], cfg["D"]), cfg)
    if sim:
        from simrun import simulate
        results = simulate(nc, in_maps[:sim_cores], ["out"], num_cores=sim_cores)
    else:
        results = run_bass_kernel_spmd(
            nc, in_maps, core_ids=core_ids or list(range(8))).results
    B = x.shape[0]
    out = np.zeros((B, cfg["T"], cfg["D"]), dtype=np.float32)
    for c in range(len(results)):
        b, gmap = gmaps[c]
        out[b][gmap] = results[c]["out"]
    return out


def kernel(L0_attn_x, L0_attn_bv_q, L0_attn_bv_k, L0_attn_bv_v,
           L0_ffn_base_W, L0_ffn_gate_W, L0_ffn_lora_A, L0_ffn_lora_B):
    cfg = make_cfg(T=4096, D=4096)
    return run(cfg, np.asarray(L0_attn_x), np.asarray(L0_attn_bv_q),
               np.asarray(L0_attn_bv_k), np.asarray(L0_attn_bv_v),
               np.asarray(L0_ffn_base_W), np.asarray(L0_ffn_gate_W),
               np.asarray(L0_ffn_lora_A), np.asarray(L0_ffn_lora_B))
